# version 03
# baseline (speedup 1.0000x reference)
import numpy as np
import ml_dtypes
import concourse.bass as bass
import concourse.mybir as mybir
from concourse.bass_utils import run_bass_kernel_spmd
from contextlib import ExitStack

F32 = mybir.dt.float32
BF16 = mybir.dt.bfloat16
I32 = mybir.dt.int32
AF = mybir.ActivationFunctionType
ALU = mybir.AluOpType
AX = mybir.AxisListType
BF = ml_dtypes.bfloat16

T, D = 2048, 1024
E2, NS = 2048, 16
FF, NEXP = 4096, 8
H, DH = 16, 64
NT = T // 128
TBLK = 512
NBLK = T // TBLK
EPS_LN = 1e-5
EPS_ATT = 1e-6

ENGINES = ("pe", "act", "dve", "pool", "sp")
RING_SLOTS = 3
RING_COLS = 8192
ARENA_COLS = 32896

_UPTO = "all"


def _pfm_layout():
    lay, off = {}, 0
    for name, n in [("gamma", 8), ("beta", 8), ("ln_g", 8), ("ln_b", 8),
                    ("b1", 256), ("b_in", 16), ("b_in_g", 16), ("conv_b", 16),
                    ("b_c", 16), ("b_d", 16), ("logd", 16), ("convw", 64),
                    ("A", 256), ("Bm", 256)]:
        lay[name] = off
        off += n
    return lay, off


PFM_LAYOUT, PFM_COLS = _pfm_layout()


def ceil_div(a, b):
    return (a + b - 1) // b



def pack_tiles(WT, order):
    K, M = WT.shape
    W4 = WT.reshape(K // 128, 128, M // 128, 128)
    return np.ascontiguousarray(np.stack([W4[kc, :, mc, :] for kc, mc in order]))


def modeA_order(n_k, n_m, grp):
    order = []
    for mg in range(ceil_div(n_m, grp)):
        for kc in range(n_k):
            for mc in range(mg * grp, min((mg + 1) * grp, n_m)):
                order.append((kc, mc))
    return order


def tiles_to_units(tiles, tiles_per_unit):
    n = tiles.shape[0] // tiles_per_unit
    return np.ascontiguousarray(
        tiles.reshape(n, tiles_per_unit, 128, 128).transpose(0, 2, 1, 3)
        .reshape(n, 128, tiles_per_unit * 128))


def rows_pack(WT):
    K, M = WT.shape
    return np.ascontiguousarray(
        WT.reshape(K // 128, 128, M).transpose(1, 0, 2)).reshape(1, 128, (K // 128) * M)


def fmv(v, nchunk):
    return np.ascontiguousarray(np.asarray(v, np.float32).reshape(nchunk, 128).T)


def prepare_inputs(inp):
    f32 = np.float32
    tp = lambda a: np.ascontiguousarray(np.asarray(a, f32).T)
    d = {}
    for nm in ("Wq", "Wk"):
        d[f"{nm}T_pack"] = tiles_to_units(
            pack_tiles(tp(inp[f"L0_attn_a_{nm}"]), modeA_order(8, 8, 4)), 64)
    for nm in ("Wv", "Wp"):
        d[f"{nm}T_pack"] = rows_pack(tp(inp[f"L0_attn_a_{nm}"]))
    W1 = np.asarray(inp["L0_ffn_W1"], f32)
    W2 = np.asarray(inp["L0_ffn_W2"], f32)
    d["W1_units"] = np.stack([tiles_to_units(
        pack_tiles(W1[e], modeA_order(8, 32, 2)), 64) for e in range(NEXP)])
    d["W2_units"] = np.stack([np.ascontiguousarray(
        W2[e].reshape(4, 8, 128, 1024).transpose(0, 2, 1, 3).reshape(4, 128, 8192))
        for e in range(NEXP)])
    WgT = tp(inp["L0_ffn_Wg"])
    d["WgT_fm"] = np.ascontiguousarray(
        WgT.reshape(8, 128, 8).transpose(1, 0, 2)).reshape(128, 64)
    WinT = tp(inp["L0_attn_b_W_in"])
    d["WinT_xp"] = tiles_to_units(pack_tiles(WinT[:, :E2], modeA_order(8, 16, 2)), 16)
    d["WinT_gate"] = tiles_to_units(pack_tiles(WinT[:, E2:], modeA_order(8, 16, 2)), 16)
    for nm in ("W_c", "W_d"):
        d[f"{nm}T_pack"] = tiles_to_units(
            pack_tiles(tp(inp[f"L0_attn_b_{nm}"]), modeA_order(16, 16, 2)), 32)
    d["WoT_pack"] = rows_pack(tp(inp["L0_attn_b_W_out"]))

    cols = []
    for name, arr in [
        ("gamma", fmv(inp["L0_ffn_gamma"], 8)), ("beta", fmv(inp["L0_ffn_beta"], 8)),
        ("ln_g", fmv(inp["L0_attn_b_ln_g"], 8)), ("ln_b", fmv(inp["L0_attn_b_ln_b"], 8)),
        ("b1", np.concatenate([fmv(np.asarray(inp["L0_ffn_b1"], f32)[e], 32)
                               for e in range(NEXP)], axis=1)),
        ("b_in", fmv(np.asarray(inp["L0_attn_b_b_in"], f32)[:E2], 16)),
        ("b_in_g", fmv(np.asarray(inp["L0_attn_b_b_in"], f32)[E2:], 16)),
        ("conv_b", fmv(inp["L0_attn_b_conv_b"], 16)),
        ("b_c", fmv(inp["L0_attn_b_b_c"], 16)),
        ("b_d", fmv(inp["L0_attn_b_b_d"], 16)),
        ("logd", fmv(inp["L0_attn_b_log_delta"], 16)),
        ("convw", np.concatenate([fmv(np.ascontiguousarray(
            np.asarray(inp["L0_attn_b_conv_w"], f32)[:, 0, j]), 16) for j in range(4)], axis=1)),
        ("A", np.concatenate([fmv(np.ascontiguousarray(
            np.asarray(inp["L0_attn_b_A"], f32)[:, n]), 16) for n in range(NS)], axis=1)),
        ("Bm", np.concatenate([fmv(np.ascontiguousarray(
            np.asarray(inp["L0_attn_b_Bm"], f32)[:, n]), 16) for n in range(NS)], axis=1)),
    ]:
        assert PFM_LAYOUT[name] == sum(c.shape[1] for c in cols)
        cols.append(arr)
    d["pfm_h"] = np.ascontiguousarray(np.concatenate(cols, axis=1))
    assert d["pfm_h"].shape[1] == PFM_COLS
    rows = np.concatenate([np.asarray(inp["L0_attn_a_bp"], f32),
                           np.asarray(inp["L0_attn_b_b_out"], f32),
                           np.asarray(inp["L0_ffn_b2"], f32).reshape(-1)])
    d["rows_h"] = np.ascontiguousarray(rows.reshape(1, -1))
    return d



class Rec:
    def __init__(self):
        self.thunks = {k: [] for k in ENGINES}
        self.counts = {}
        self.handles = {}
        self.last_wait = {k: {} for k in ENGINES}
        self.bank_dep = [None] * 8

    def op(self, eng, build, inc=None, dma=False, fragile=False):
        if fragile:
            self.wait(eng, self.now(eng))
            inc = inc or eng
        if inc is None:
            self.thunks[eng].append(build)
            return None
        amt = 16 if dma else 1
        self.counts[inc] = self.counts.get(inc, 0) + amt
        tgt = self.counts[inc]

        def fn(e, b=build, s=inc, a=amt):
            b(e).then_inc(self.handles[s], a)
        self.thunks[eng].append(fn)
        return (inc, tgt)

    def wait(self, eng, dep):
        if dep is None:
            return
        s, v = dep
        if v <= self.last_wait[eng].get(s, 0):
            return
        self.last_wait[eng][s] = v
        self.thunks[eng].append(lambda e, s=s, v=v: e.wait_ge(self.handles[s], v))

    def now(self, sem):
        return (sem, self.counts.get(sem, 0))

    def sem_names(self):
        names = set(self.counts.keys())
        names.update(["pe", "act", "dve", "pool", "ld", "in", "st"])
        return sorted(names)


class Arena:
    def __init__(self, cols):
        self.cols = cols
        self.off = 0
        self.marks = []

    def push(self):
        self.marks.append(self.off)

    def pop(self):
        self.off = self.marks.pop()

    def alloc(self, n):
        n = (n + 1) & ~1
        assert self.off + n <= self.cols, f"arena overflow {self.off}+{n}"
        o = self.off
        self.off += n
        return o



class Kern:
    def __init__(self, upto="all", sub=99, core_n=9999, core_p=99):
        self.upto = upto
        self.sub = sub
        self.core_n = core_n
        self.core_p = core_p
        self.core_mask = True
        self.dbg = None
        self.nc = bass.Bass("TRN2", target_bir_lowering=False,
                            detect_race_conditions=False)
        self.rec = Rec()
        self.stack = ExitStack()
        self.ring_turn = 0
        self.ring_dep = [None] * RING_SLOTS
        self.tr_turn = 0
        self.stream_dep = [None] * NT
        self.z1_act_rd = None

    def po(self, name):
        return PFM_LAYOUT[name]

    def abf(self, off, n):
        return self.arena_t[:, off:off + n]

    def af32(self, off, n):
        return self.arena_t[:].bitcast(F32)[:, off // 2:off // 2 + n]

    def bank_bf(self, i):
        return self.pb[i][:].bitcast(BF16)

    def wait_bank(self, eng, i):
        self.rec.wait(eng, self.rec.bank_dep[i])

    def use_bank(self, i, dep):
        self.rec.bank_dep[i] = dep

    def ring_load(self, src_ap, n_cols=None):
        R = self.rec
        if n_cols is None:
            n_cols = src_ap.shape[-1]
        slot = self.ring_turn % RING_SLOTS
        self.ring_turn += 1
        off = slot * RING_COLS
        R.wait("pool", self.ring_dep[slot])
        dep = R.op("pool", lambda e, s=src_ap, o=off, n=n_cols:
                   e.dma_start(out=self.ring[:, o:o + n], in_=s), inc="ld", dma=True)
        return off, dep, slot

    def barrier(self):
        R = self.rec
        deps = {e: R.now(e) for e in ("pe", "act", "dve")}
        for e in ("pe", "act", "dve"):
            for o in ("pe", "act", "dve"):
                if o != e:
                    R.wait(e, deps[o])


    def modeA(self, n_k, n_m, grp, stat_tile, moving, evac, ring_units, unit_src,
              tiles_per_unit_grp):
        R = self.rec
        n_grp = ceil_div(n_m, grp)
        grp_per_unit = tiles_per_unit_grp
        u_off = u_dep = u_slot = None
        last_pe_for_slot = None
        for mg in range(n_grp):
            if mg % grp_per_unit == 0:
                if u_slot is not None:
                    self.ring_dep[u_slot] = last_pe_for_slot
                u = mg // grp_per_unit
                u_off, u_dep, u_slot = self.ring_load(unit_src(u), None)
            mcs = list(range(mg * grp, min((mg + 1) * grp, n_m)))
            base = 0 if (mg % 2 == 0) else 4
            banks = [base + i for i in range(len(mcs))]
            for b in banks:
                self.wait_bank("pe", b)
            R.wait("pe", u_dep)
            mm = None
            for kc in range(n_k):
                for i, mc in enumerate(mcs):
                    t = stat_tile(u_off, kc, mc)
                    mv = moving(kc)
                    mm = R.op("pe", lambda e, t=t, b=banks[i], mv=mv, s=(kc == 0), p=(kc == n_k - 1):
                              e.matmul(out=self.pb[b][:], lhsT=self.ring[:, t:t + 128],
                                       rhs=mv, start=s, stop=p),
                              inc="pe" if (kc == n_k - 1 and i == len(mcs) - 1) else None)
            last_pe_for_slot = mm
            for i, mc in enumerate(mcs):
                dep = evac(mc, banks[i], mm)
                self.use_bank(banks[i], dep)
        if u_slot is not None:
            self.ring_dep[u_slot] = R.now("pe")

    def modeB(self, n_k, stat, moving, t0locs, bias_cols, evac, ring_units, unit_src,
              k_per_unit):
        R = self.rec
        for b in range(2 * len(t0locs)):
            self.wait_bank("pe", b)
        has_bias = bias_cols is not None
        u_off = u_dep = u_slot = None
        mmlast = None
        for kc in range(n_k):
            if kc % k_per_unit == 0:
                if u_slot is not None:
                    self.ring_dep[u_slot] = mmlast
                u = kc // k_per_unit
                u_off, u_dep, u_slot = self.ring_load(unit_src(u), None)
                R.wait("pe", u_dep)
            unit_end = (kc % k_per_unit == k_per_unit - 1) or kc == n_k - 1
            for t0i, tl in enumerate(t0locs):
                s = stat(kc, tl)
                for dh in range(2):
                    mv = moving(u_off, kc, dh)
                    final = (kc == n_k - 1) and not has_bias
                    mark = unit_end and t0i == len(t0locs) - 1 and dh == 1
                    mmlast = R.op("pe", lambda e, s=s, mv=mv, b=2 * t0i + dh,
                                  st=(kc == 0), sp=final:
                                  e.matmul(out=self.pb[b][:], lhsT=s, rhs=mv,
                                           start=st, stop=sp),
                                  inc="pe" if mark else None)
        if u_slot is not None:
            self.ring_dep[u_slot] = mmlast
        if has_bias:
            R.wait("pe", self.dep_rows)
            for t0i in range(len(t0locs)):
                for dh in range(2):
                    ro = bias_cols[dh]
                    mmlast = R.op("pe", lambda e, ro=ro, b=2 * t0i + dh: e.matmul(
                        out=self.pb[b][:], lhsT=self.ones1r[:],
                        rhs=self.rows_bf[0:1, ro:ro + 512], start=False, stop=True),
                        inc="pe" if (t0i == len(t0locs) - 1 and dh == 1) else None)
        for t0i, tl in enumerate(t0locs):
            for dh in range(2):
                dep = evac(tl, dh, 2 * t0i + dh, mmlast)
                self.use_bank(2 * t0i + dh, dep)


    def build(self):
        nc, R = self.nc, self.rec
        E = self.stack.enter_context
        self.inp = {}
        self.x_hbm = nc.dram_tensor("x", [T, D], F32, kind="ExternalInput").ap()
        self.out_hbm = nc.dram_tensor("out", [T, D], F32, kind="ExternalOutput").ap()
        for nm, shape in [
            ("WqT_pack", (1, 128, 8192)), ("WkT_pack", (1, 128, 8192)),
            ("WvT_pack", (1, 128, 8192)), ("WpT_pack", (1, 128, 8192)),
            ("W1_units", (8, 4, 128, 8192)), ("W2_units", (8, 4, 128, 8192)),
            ("WgT_fm", (128, 64)),
            ("WinT_xp", (8, 128, 2048)), ("WinT_gate", (8, 128, 2048)),
            ("W_cT_pack", (8, 128, 4096)), ("W_dT_pack", (8, 128, 4096)),
            ("WoT_pack", (1, 128, 16384)),
            ("pfm_h", (128, PFM_COLS)), ("rows_h", (1, 10240)),
        ]:
            self.inp[nm] = nc.dram_tensor(nm, list(shape), F32, kind="ExternalInput").ap()

        self.xs = E(nc.sbuf_tensor("xs", [128, NT * 1024], F32))
        self.ident = E(nc.sbuf_tensor("ident", [128, 128], BF16))
        self.mask4 = E(nc.sbuf_tensor("mask4", [128, 512], BF16))
        self.iota_i = E(nc.sbuf_tensor("iota_i", [128, 128], I32))
        self.iota_f = E(nc.sbuf_tensor("iota_f", [128, 128], F32))
        self.ones1r = E(nc.sbuf_tensor("ones1r", [1, 128], BF16))
        self.rows_bf = E(nc.sbuf_tensor("rows_bf", [1, 10240], BF16))
        self.pfm = E(nc.sbuf_tensor("pfm", [128, PFM_COLS], F32))
        self.pext = E(nc.sbuf_tensor("pext", [128, 32], F32))
        self.wg_bf = E(nc.sbuf_tensor("wg_bf", [128, 64], BF16))
        self.S_all = E(nc.sbuf_tensor("S_all", [128, 520], F32))
        self.S_bf = E(nc.sbuf_tensor("S_bf", [128, 1040], BF16))
        self.vcar = E(nc.sbuf_tensor("vcar", [128, 16], F32))
        self.xptail = E(nc.sbuf_tensor("xptail", [128, 48], F32))
        self.G_blk = E(nc.sbuf_tensor("G_blk", [128, 64], F32))
        self.stat = E(nc.sbuf_tensor("stat", [128, 96], F32))
        self.gsc = E(nc.sbuf_tensor("gsc", [128, 24], F32))
        self.ring = E(nc.sbuf_tensor("ring", [128, RING_SLOTS * RING_COLS], BF16))
        self.arena_t = E(nc.sbuf_tensor("arena", [128, ARENA_COLS], BF16))
        self.arena = Arena(ARENA_COLS)
        self.pb = [E(nc.psum_tensor(f"pb{i}", [128, 512], F32)) for i in range(8)]

        self.emit_init()
        self.emit_attn_a()
        if self.upto != "attn_a":
            self.emit_ffn(0)
            if self.upto not in ("ffn1",):
                self.emit_attn_b()
                if self.upto not in ("attn_b",):
                    self.emit_ffn(1)
        self.emit_store()

        for s in R.sem_names():
            R.handles[s] = E(nc.semaphore(s))
        blk = E(nc.Block())
        th = R.thunks

        @blk.tensor
        def _(e):
            for fn in th["pe"]:
                fn(e)

        @blk.scalar
        def _(e):
            for fn in th["act"]:
                fn(e)

        @blk.vector
        def _(e):
            for fn in th["dve"]:
                fn(e)

        @blk.gpsimd
        def _(e):
            for fn in th["pool"]:
                fn(e)

        @blk.sync
        def _(e):
            for fn in th["sp"]:
                fn(e)

        self.stack.close()
        return nc


    def emit_init(self):
        R = self.rec
        xr = self.x_hbm.rearrange("(t p) d -> p t d", p=128)
        xsv = self.xs[:].rearrange("p (t d) -> p t d", t=NT)
        self.dep_x = R.op("sp", lambda e: e.dma_start(out=xsv, in_=xr),
                          inc="in", dma=True)
        d1 = R.op("sp", lambda e: e.dma_start(out=self.pfm[:], in_=self.inp["pfm_h"]),
                  inc="in", dma=True)
        self.dep_pfm = d1
        self.dep_rows = R.op("pool", lambda e: e.dma_start(
            out=self.rows_bf[:], in_=self.inp["rows_h"]), inc="ld", dma=True)
        self.dep_wg = R.op("pool", lambda e: e.dma_start(
            out=self.wg_bf[:], in_=self.inp["WgT_fm"]), inc="ld", dma=True)
        di = R.op("pool", lambda e: e.iota(self.iota_i[:], pattern=[[1, 128]], base=0,
                                           channel_multiplier=-1), inc="pool")
        R.wait("dve", di)
        R.op("dve", lambda e: e.tensor_copy(out=self.iota_f[:], in_=self.iota_i[:]))
        R.op("dve", lambda e: e.tensor_single_scalar(
            out=self.ident[:], in_=self.iota_f[:], scalar=0.0, op=ALU.is_equal))
        for j in range(4):
            R.op("dve", lambda e, j=j: e.tensor_single_scalar(
                out=self.mask4[:, 128 * j:128 * j + 128], in_=self.iota_f[:],
                scalar=0.0, op=ALU.is_ge))
        R.op("dve", lambda e: e.memset(self.ones1r[:], 1.0))
        R.op("dve", lambda e: e.memset(self.stat[:], 0.0))
        R.op("dve", lambda e: e.memset(self.gsc[:], 0.0))
        R.op("dve", lambda e: e.memset(self.S_all[:], 0.0))
        R.op("dve", lambda e: e.memset(self.S_bf[:], 0.0))
        R.op("dve", lambda e: e.memset(self.vcar[:], 0.0))
        R.op("dve", lambda e: e.memset(self.xptail[:], 0.0))
        R.wait("act", d1)
        da = R.op("act", lambda e: e.activation(
            out=self.pext[:, 0:16], in_=self.pfm[:, self.po("logd"):self.po("logd") + 16],
            func=AF.Sigmoid, scale=-1.0), inc="act")
        R.wait("dve", da)
        R.wait("dve", d1)
        R.op("dve", lambda e: e.memset(self.pext[:, 16:32], 0.0))
        for n in range(NS):
            ao, bo = self.po("A") + 16 * n, self.po("Bm") + 16 * n
            R.op("dve", lambda e, ao=ao, bo=bo: e.tensor_tensor(
                out=self.stat[:, 0:16], in0=self.pfm[:, ao:ao + 16],
                in1=self.pfm[:, bo:bo + 16], op=ALU.mult), fragile=True)
            R.op("dve", lambda e: e.tensor_tensor(
                out=self.pext[:, 16:32], in0=self.pext[:, 16:32],
                in1=self.stat[:, 0:16], op=ALU.add), fragile=True)
        self.dep_const = R.op("dve", lambda e: e.nop(), inc="dve")


    def ln_tile(self, t0, z1_off, sq_off, jnk_off, sc, fold_residual, double):
        R = self.rec
        st = self.stat
        xt = self.xs[:, t0 * 1024:(t0 + 1) * 1024]
        z1 = self.af32(z1_off, 1024)
        sq = self.af32(sq_off, 1024)
        jnk = self.af32(jnk_off, 1024)

        def stats(src, cm, cr, src_dep_dve, src_dep_act):
            R.wait("dve", src_dep_dve)
            R.op("dve", lambda e: e.tensor_scalar(
                out=jnk, in0=src, scalar1=-1.0 / D, scalar2=0.0,
                op0=ALU.mult, op1=ALU.add, accum_out=st[:, cm:cm + 1]), fragile=True)
            R.wait("act", self.dep_const)
            R.wait("act", src_dep_act)
            db = R.op("act", lambda e: e.activation(
                out=sq, in_=src, func=AF.Square, scale=1.0 / 32.0,
                accum_out=st[:, cr + 1:cr + 2]), inc="act")
            R.wait("dve", db)
            R.op("dve", lambda e: e.tensor_tensor(
                out=st[:, cr:cr + 1], in0=st[:, cm:cm + 1], in1=st[:, cm:cm + 1],
                op=ALU.mult), fragile=True)
            R.op("dve", lambda e: e.tensor_scalar(
                out=st[:, cr:cr + 1], in0=st[:, cr + 1:cr + 2],
                scalar1=st[:, cr:cr + 1], scalar2=EPS_LN,
                op0=ALU.subtract, op1=ALU.add), fragile=True)
            dc = R.op("dve", lambda e: e.reciprocal(
                out=st[:, cr:cr + 1], in_=st[:, cr:cr + 1]), inc="dve", fragile=True)
            R.wait("act", dc)
            dd = R.op("act", lambda e: e.activation(
                out=st[:, cr:cr + 1], in_=st[:, cr:cr + 1], func=AF.Sqrt), inc="act")
            R.wait("dve", dd)
            return db

        sdep = self.stream_dep[t0] or self.dep_x
        db1 = stats(xt, sc, sc + 1, sdep, sdep)
        R.wait("dve", self.z1_act_rd)
        dz1 = R.op("dve", lambda e: e.tensor_scalar(
            out=z1, in0=xt, scalar1=st[:, sc:sc + 1], scalar2=st[:, sc + 1:sc + 2],
            op0=ALU.add, op1=ALU.mult), inc="dve", fragile=True)
        if fold_residual:
            R.wait("dve", db1)
            dres = R.op("dve", lambda e: e.tensor_tensor(
                out=xt, in0=xt, in1=z1, op=ALU.add), inc="dve")
            self.stream_dep[t0] = dres
        if double:
            db2 = stats(z1, sc + 3, sc + 4, None, dz1)
            self.z1_act_rd = db2
            R.wait("dve", db2)
            dz2 = R.op("dve", lambda e: e.tensor_scalar(
                out=z1, in0=z1, scalar1=st[:, sc + 3:sc + 4], scalar2=st[:, sc + 4:sc + 5],
                op0=ALU.add, op1=ALU.mult), inc="dve", fragile=True)
            return dz2
        return dz1

    def ln_transpose(self, t0loc, zb_off, dst_off, dst_stride, gcol, bcol, dep_zb):
        R = self.rec
        dd = None
        for dc in range(8):
            bk = 6 + (self.tr_turn % 2)
            self.tr_turn += 1
            self.wait_bank("pe", bk)
            R.wait("pe", dep_zb)
            dt = R.op("pe", lambda e, dc=dc, bk=bk: e.transpose(
                out=self.bank_bf(bk)[:, 0:128], in_=self.abf(zb_off + 128 * dc, 128),
                identity=self.ident[:]), inc="pe")
            R.wait("dve", dt)
            do = dst_off + dc * dst_stride + 128 * t0loc
            if gcol is None:
                dd = R.op("dve", lambda e, do=do, bk=bk: e.tensor_copy(
                    out=self.abf(do, 128), in_=self.bank_bf(bk)[:, 0:128]), inc="dve")
            else:
                dd = R.op("dve", lambda e, do=do, bk=bk, dc=dc: e.tensor_scalar(
                    out=self.abf(do, 128), in0=self.bank_bf(bk)[:, 0:128],
                    scalar1=self.pfm[:, gcol + dc:gcol + dc + 1],
                    scalar2=self.pfm[:, bcol + dc:bcol + dc + 1],
                    op0=ALU.mult, op1=ALU.add), inc="dve")
            self.use_bank(bk, dd)
        return dd

    def ln_block(self, blk, z1_off, sq_off, jnk_off, zb_off, dst_off, gcol, bcol,
                 fold, double):
        R = self.rec
        for tl in range(4):
            t0 = blk * 4 + tl
            dz = self.ln_tile(t0, z1_off, sq_off, jnk_off, 8 * (t0 % 8), fold, double)
            R.wait("dve", self.zb_rd)
            dzb = R.op("dve", lambda e: e.tensor_copy(
                out=self.abf(zb_off, 1024), in_=self.af32(z1_off, 1024)), inc="dve")
            dlast = self.ln_transpose(tl, zb_off, dst_off, 512, gcol, bcol, dzb)
            self.zb_rd = R.now("pe")
        return R.now("dve")

    def ln_stats_block(self, blk, z1_off, sq_off, jnk_off, zblk_off, fold, double):
        R = self.rec
        deps = []
        for tl in range(4):
            t0 = blk * 4 + tl
            dz = self.ln_tile(t0, z1_off, sq_off, jnk_off, 8 * (t0 % 8), fold, double)
            R.wait("dve", self.zblk_rd)
            dzb = R.op("dve", lambda e, tl=tl: e.tensor_copy(
                out=self.abf(zblk_off + 1024 * tl, 1024), in_=self.af32(z1_off, 1024)),
                inc="dve")
            deps.append(dzb)
        return deps

    def ln_tr_block(self, zblk_off, dst_off, gcol, bcol, dzbs):
        R = self.rec
        for tl in range(4):
            self.ln_transpose(tl, zblk_off + 1024 * tl, dst_off, 512, gcol, bcol,
                              dzbs[tl])
        self.zblk_rd = R.now("pe")
        return R.now("dve")


    def emit_attn_a(self):
        R = self.rec
        A = self.arena
        A.push()
        z1_off = A.alloc(2048)
        sq_off = A.alloc(2048)
        jnk_off = A.alloc(2048)
        zb_off = A.alloc(1024)
        y0_off = A.alloc(4096)
        pq_off = A.alloc(8192)
        pk_off = A.alloc(4096)
        v_off = A.alloc(4 * 1040)
        at_off = A.alloc(2048)
        pkt_off = A.alloc(2048)
        yt_off = A.alloc(1024)
        yf_off = y0_off
        self.zb_rd = None
        self.phi_prev = (None, None)
        ph1, ph2 = z1_off, sq_off
        R.wait("dve", self.dep_const)
        R.op("dve", lambda e: e.memset(self.abf(pq_off, 8192), 0.0))
        self.dep_sbf = R.op("dve", lambda e: e.nop(), inc="dve")

        for blk in range(NBLK):
            if self.sub < 1:
                break
            dep_y0 = self.ln_block(blk, z1_off, sq_off, jnk_off, zb_off, y0_off,
                                   None, None, fold=False, double=False)
            if self.sub < 2:
                continue

            for nm, dst in (("WqT_pack", pq_off), ("WkT_pack", pk_off)):
                def phi_evac(mc, bank, mm_dep, dst=dst, nm=nm):
                    pm = self.af32(ph1, 512)
                    pr = self.af32(ph2, 512)
                    pex = self.af32(jnk_off, 512)
                    prev_d2, prev_d3 = self.phi_prev
                    R.wait("dve", mm_dep)
                    R.wait("dve", prev_d2)
                    R.op("dve", lambda e: e.tensor_single_scalar(
                        out=pm, in_=self.pb[bank][:], scalar=0.0, op=ALU.min))
                    d1 = R.op("dve", lambda e: e.tensor_single_scalar(
                        out=pr, in_=self.pb[bank][:], scalar=0.0, op=ALU.max), inc="dve")
                    R.wait("act", d1)
                    R.wait("act", prev_d3)
                    d2 = R.op("act", lambda e: e.activation(
                        out=pex, in_=pm, func=AF.Exp), inc="act")
                    R.wait("dve", d2)
                    if nm == "WqT_pack":
                        R.op("dve", lambda e, mc=mc: e.tensor_tensor(
                            out=self.abf(dst + (2 * mc) * 512, 512)[0:64, :],
                            in0=pex[0:64, :], in1=pr[0:64, :], op=ALU.add))
                        d3 = R.op("dve", lambda e, mc=mc: e.tensor_tensor(
                            out=self.abf(dst + (2 * mc + 1) * 512, 512)[64:128, :],
                            in0=pex[64:128, :], in1=pr[64:128, :], op=ALU.add),
                            inc="dve")
                    else:
                        d3 = R.op("dve", lambda e, mc=mc: e.tensor_tensor(
                            out=self.abf(dst + mc * 512, 512), in0=pex, in1=pr,
                            op=ALU.add), inc="dve")
                    self.phi_prev = (d2, d3)
                    return d1
                self.phi_prev = (None, None)
                R.wait("pe", dep_y0)
                self.modeA(8, 8, 4,
                           stat_tile=lambda u_off, kc, mc: u_off + (((mc // 4) * 8 + kc) * 4 + mc % 4) * 128,
                           moving=lambda kc: self.abf(y0_off + kc * 512, 512),
                           evac=phi_evac,
                           ring_units=1, unit_src=lambda u, nm=nm: self.inp[nm][0],
                           tiles_per_unit_grp=2)
            dep_phi = R.now("dve")
            if self.sub < 3:
                continue

            dms = R.op("dve", lambda e: e.memset(
                self.abf(v_off, 4160).rearrange("p (n h c) -> p n h c", n=4, h=16)[:, :, :, 64:65],
                1.0), inc="dve")

            def v_evac(tl, dh, bank, dep):
                R.wait("dve", dep)
                dstv = self.abf(v_off + tl * 1040, 1040).rearrange(
                    "p (h c) -> p h c", c=65)[:, 8 * dh:8 * dh + 8, 0:64]
                srcv = self.pb[bank][:].rearrange("p (h c) -> p h c", c=64)
                dd = R.op("dve", lambda e: e.tensor_copy(out=dstv, in_=srcv), inc="dve")
                return dd
            R.wait("pe", dep_y0)
            self.modeB(8, stat=lambda kc, tl: self.abf(y0_off + kc * 512 + tl * 128, 128),
                       moving=lambda u_off, kc, dh: self.ring[:, u_off + kc * 1024 + dh * 512:
                                                             u_off + kc * 1024 + dh * 512 + 512],
                       t0locs=list(range(4)), bias_cols=None,
                       evac=v_evac, ring_units=1,
                       unit_src=lambda u: self.inp["WvT_pack"][0], k_per_unit=8)
            dep_v = R.now("dve")
            if self.sub < 4:
                continue

            for nl in range(4):
                for w in range(2):
                    if blk * 8 + nl * 2 + w >= self.core_n:
                        continue
                    self.attn_core_wave(blk, nl, w, pq_off, pk_off, v_off,
                                        at_off, pkt_off, yt_off, yf_off)

            if self.sub < 5:
                continue
            def o_evac(tl, dh, bank, dep):
                t0 = blk * 4 + tl
                xt = self.xs[:, t0 * 1024 + dh * 512: t0 * 1024 + dh * 512 + 512]
                R.wait("dve", dep)
                dd = R.op("dve", lambda e, xt=xt, bank=bank: e.tensor_tensor(
                    out=xt, in0=xt, in1=self.pb[bank][:], op=ALU.add), inc="dve")
                self.stream_dep[t0] = dd
                return dd
            self.modeB(8, stat=lambda kc, tl: self.abf(yf_off + kc * 512 + tl * 128, 128),
                       moving=lambda u_off, kc, dh: self.ring[:, u_off + kc * 1024 + dh * 512:
                                                             u_off + kc * 1024 + dh * 512 + 512],
                       t0locs=list(range(4)), bias_cols=(0, 512),
                       evac=o_evac, ring_units=1,
                       unit_src=lambda u: self.inp["WpT_pack"][0], k_per_unit=8)
            self.barrier()
        A.pop()

    def attn_core_wave(self, blk, nl, w, pq_off, pk_off, v_off, at_off, pkt_off,
                       yt_off, yf_off):
        R = self.rec
        rot = (nl * 2 + w) % 2
        at = at_off + rot * 1024
        pkt = pkt_off + rot * 1024
        yt = yt_off + rot * 512
        bA = [0, 1] if rot == 0 else [2, 3]
        bKV = 4 if rot == 0 else 5
        bY = [0, 1] if rot == 1 else [2, 3]

        tb = 6 + (self.tr_turn % 2)
        self.tr_turn += 1
        self.wait_bank("pe", tb)
        dt = None
        for i in range(4):
            ec = 4 * w + i
            dt = R.op("pe", lambda e, ec=ec, i=i, tb=tb: e.transpose(
                out=self.bank_bf(tb)[:, i * 128:i * 128 + 128],
                in_=self.abf(pk_off + ec * 512 + nl * 128, 128),
                identity=self.ident[:]), inc="pe" if i == 3 else None)
        R.wait("dve", dt)
        dpk = R.op("dve", lambda e, tb=tb: e.tensor_copy(
            out=self.abf(pkt, 512), in_=self.bank_bf(tb)[:, 0:512]), inc="dve")
        self.use_bank(tb, dpk)
        if self.core_p < 2:
            return

        for b in bA:
            self.wait_bank("pe", b)
        mmA = None
        for i in range(8):
            h = 8 * w + i
            ec = h // 2
            lk = self.abf(pk_off + ec * 512 + nl * 128, 128)
            rq = self.abf(pq_off + h * 512 + nl * 128, 128)
            mmA = R.op("pe", lambda e, lk=lk, rq=rq, b=bA[i // 4], c=(i % 4) * 128:
                       e.matmul(out=self.pb[b][:, c:c + 128], lhsT=lk, rhs=rq,
                                start=True, stop=True),
                       inc="pe" if i == 7 else None)
        R.wait("dve", mmA)
        dat = None
        for j in range(2):
            if self.core_mask:
                dat = R.op("dve", lambda e, j=j: e.tensor_tensor(
                    out=self.abf(at + j * 512, 512), in0=self.pb[bA[j]][:],
                    in1=self.mask4[:], op=ALU.mult), inc="dve")
            else:
                dat = R.op("dve", lambda e, j=j: e.tensor_copy(
                    out=self.abf(at + j * 512, 512), in_=self.pb[bA[j]][:]), inc="dve")
        for b in bA:
            self.use_bank(b, dat)
        if self.core_p < 3:
            return

        self.wait_bank("pe", bKV)
        R.wait("pe", dpk)
        mmKV = None
        for i in range(8):
            h = 8 * w + i
            lh = self.abf(pkt + i * 64, 64)
            rv = self.abf(v_off + nl * 1040 + h * 65, 65)
            ro, co = 64 * (i % 2), (i // 2) * 65
            mmKV = R.op("pe", lambda e, lh=lh, rv=rv, ro=ro, co=co, b=bKV: e.matmul(
                out=self.pb[b][ro:ro + 64, co:co + 65], lhsT=lh, rhs=rv,
                start=True, stop=True), inc="pe" if i == 7 else None)

        if self.core_p < 4:
            dS0 = R.op("dve", lambda e, b=bKV: e.tensor_copy(
                out=self.abf(pkt, 260), in_=self.pb[b][:, 0:260]), inc="dve")
            self.use_bank(bKV, dS0)
            return
        for b in bY:
            self.wait_bank("pe", b)
        R.wait("pe", dat)
        R.wait("pe", self.dep_sbf)
        mmY = None
        for i in range(8):
            h = 8 * w + i
            ec, ro = h // 2, 64 * (h % 2)
            b, co = bY[i // 4], (i % 4) * 65
            la = self.abf(at + i * 128, 128)
            rv = self.abf(v_off + nl * 1040 + h * 65, 65)
            R.op("pe", lambda e, la=la, rv=rv, b=b, co=co: e.matmul(
                out=self.pb[b][:, co:co + 65], lhsT=la, rhs=rv, start=True, stop=False))
            lq = self.abf(pq_off + h * 512 + nl * 128, 128)
            rs = self.S_bf[:, h * 65:h * 65 + 65]
            mmY = R.op("pe", lambda e, lq=lq, rs=rs, b=b, co=co: e.matmul(
                out=self.pb[b][:, co:co + 65], lhsT=lq, rhs=rs, start=False, stop=True),
                inc="pe" if i == 7 else None)

        if self.core_p < 5:
            dS0 = R.op("dve", lambda e, b=bKV: e.tensor_copy(
                out=self.abf(pkt, 260), in_=self.pb[b][:, 0:260]), inc="dve")
            self.use_bank(bKV, dS0)
            dY0 = R.op("dve", lambda e, b=bY[0]: e.tensor_copy(
                out=self.abf(yt, 260), in_=self.pb[b][:, 0:260]), inc="dve")
            dY1 = R.op("dve", lambda e, b=bY[1]: e.tensor_copy(
                out=self.abf(yt, 260), in_=self.pb[b][:, 0:260]), inc="dve")
            self.use_bank(bY[0], dY1)
            self.use_bank(bY[1], dY1)
            return
        R.wait("dve", mmKV)
        R.wait("dve", mmY)
        so = 260 * w
        dS = R.op("dve", lambda e, so=so, b=bKV: e.tensor_tensor(
            out=self.S_all[:, so:so + 260], in0=self.S_all[:, so:so + 260],
            in1=self.pb[b][:, 0:260], op=ALU.add), inc="dve")
        self.use_bank(bKV, dS)
        for par in range(2):
            srcv = self.S_all[64 * par:64 * par + 64, so:so + 260].rearrange(
                "p (j c) -> p j c", c=65)
            dstv = self.S_bf[64 * par:64 * par + 64, :].rearrange(
                "p (h c) -> p h c", c=65)[:, 8 * w + par:8 * w + 8:2, :]
            dSb = R.op("dve", lambda e, srcv=srcv, dstv=dstv: e.tensor_copy(
                out=dstv, in_=srcv), inc="dve")
        self.dep_sbf = dSb

        if self.core_p < 6:
            return
        for j in range(2):
            R.op("dve", lambda e, j=j: e.tensor_copy(
                out=self.gsc[:, 4 * j:4 * j + 4],
                in_=self.pb[bY[j]][:, 0:260].rearrange("p (h c) -> p h c", c=65)
                [:, :, 64:65].rearrange("p h c -> p (h c)")), fragile=True)
        R.op("dve", lambda e: e.tensor_scalar(
            out=self.gsc[:, 0:8], in0=self.gsc[:, 0:8], scalar1=EPS_ATT, scalar2=0.0,
            op0=ALU.add, op1=ALU.add), fragile=True)
        R.op("dve", lambda e: e.reciprocal(out=self.gsc[:, 0:8], in_=self.gsc[:, 0:8]),
             fragile=True)
        dyt = None
        for i in range(8):
            b, co = bY[i // 4], (i % 4) * 65
            dyt = R.op("dve", lambda e, b=b, co=co, i=i: e.tensor_scalar(
                out=self.abf(yt + i * 64, 64), in0=self.pb[b][:, co:co + 64],
                scalar1=self.gsc[:, i:i + 1], scalar2=0.0,
                op0=ALU.mult, op1=ALU.add), inc="dve",
                fragile=(i == 0))
        for b in bY:
            self.use_bank(b, dyt)

        if self.core_p < 7:
            return
        tb2 = 6 + (self.tr_turn % 2)
        self.tr_turn += 1
        self.wait_bank("pe", tb2)
        R.wait("pe", dyt)
        dty = None
        for j in range(4):
            dty = R.op("pe", lambda e, j=j, tb2=tb2: e.transpose(
                out=self.bank_bf(tb2)[:, j * 128:j * 128 + 128],
                in_=self.abf(yt + j * 128, 128), identity=self.ident[:]),
                inc="pe" if j == 3 else None)
        R.wait("dve", dty)
        dyf = None
        for j in range(4):
            ec = 4 * w + j
            dyf = R.op("dve", lambda e, j=j, ec=ec, tb2=tb2: e.tensor_copy(
                out=self.abf(yf_off + ec * 512 + nl * 128, 128),
                in_=self.bank_bf(tb2)[:, j * 128:j * 128 + 128]),
                inc="dve" if j == 3 else None)
        self.use_bank(tb2, dyf)


    def emit_ffn(self, idx):
        R = self.rec
        A = self.arena
        A.push()
        z1_off = A.alloc(2048)
        sq_off = A.alloc(2048)
        jnk_off = A.alloc(2048)
        zb_off = A.alloc(1024)
        xn_off = A.alloc(4096)
        h_off = A.alloc(16384)
        pr_off = A.alloc(16)
        mk_off = A.alloc(16)
        ms_off = A.alloc(16)
        self.zb_rd = None

        for blk in range(NBLK):
            dep_xn = self.ln_block(blk, z1_off, sq_off, jnk_off, zb_off, xn_off,
                                   self.po("gamma"), self.po("beta"),
                                   fold=True, double=True)
            R.wait("pe", self.dep_wg)
            for tl in range(4):
                self.emit_gating(blk, tl, xn_off, pr_off, mk_off, ms_off, dep_xn)
            for ex in range(NEXP):
                self.emit_expert(idx, blk, ex, xn_off, h_off, dep_xn)
            self.barrier()
        A.pop()

    def emit_gating(self, blk, tl, xn_off, pr_off, mk_off, ms_off, dep_xn):
        R = self.rec
        bank = 5
        self.wait_bank("pe", bank)
        R.wait("pe", dep_xn)
        mm = None
        for kc in range(8):
            s = self.abf(xn_off + kc * 512 + tl * 128, 128)
            mm = R.op("pe", lambda e, s=s, kc=kc: e.matmul(
                out=self.pb[bank][:, 0:8], lhsT=s, rhs=self.wg_bf[:, kc * 8:kc * 8 + 8],
                start=(kc == 0), stop=(kc == 7)), inc="pe" if kc == 7 else None)
        st = self.gsc
        c0, c1, c2, c3, c4 = 8, 9, 10, 11, 12
        prob = self.af32(pr_off, 8)
        msk = self.af32(mk_off, 8)
        msd = self.af32(ms_off, 8)
        R.wait("dve", mm)
        d0 = R.op("dve", lambda e: e.reduce_max(
            out=st[:, c0:c0 + 1], in_=self.pb[bank][:, 0:8], axis=AX.X, negate=True),
            inc="dve")
        R.wait("act", d0)
        d1 = R.op("act", lambda e: e.activation(
            out=prob, in_=self.pb[bank][:, 0:8], func=AF.Exp,
            bias=st[:, c0:c0 + 1]), inc="act")
        self.use_bank(bank, d1)
        R.wait("dve", d1)
        R.op("dve", lambda e: e.reduce_sum(out=st[:, c1:c1 + 1], in_=prob, axis=AX.X), fragile=True)
        R.op("dve", lambda e: e.reciprocal(out=st[:, c1:c1 + 1], in_=st[:, c1:c1 + 1]), fragile=True)
        R.op("dve", lambda e: e.tensor_scalar(
            out=prob, in0=prob, scalar1=st[:, c1:c1 + 1], scalar2=0.0,
            op0=ALU.mult, op1=ALU.add), fragile=True)
        R.op("dve", lambda e: e.reduce_max(out=st[:, c2:c2 + 1], in_=prob, axis=AX.X), fragile=True)
        R.op("dve", lambda e: e.tensor_scalar(
            out=msk, in0=prob, scalar1=st[:, c2:c2 + 1], scalar2=0.0,
            op0=ALU.is_equal, op1=ALU.add), fragile=True)
        R.op("dve", lambda e: e.scalar_tensor_tensor(
            out=msd, in0=msk, scalar=-1e30, in1=prob, op0=ALU.mult, op1=ALU.add), fragile=True)
        R.op("dve", lambda e: e.reduce_max(out=st[:, c3:c3 + 1], in_=msd, axis=AX.X), fragile=True)
        R.op("dve", lambda e: e.tensor_scalar(
            out=msd, in0=msd, scalar1=st[:, c3:c3 + 1], scalar2=0.0,
            op0=ALU.is_equal, op1=ALU.add), fragile=True)
        R.op("dve", lambda e: e.tensor_tensor(
            out=st[:, c4:c4 + 1], in0=st[:, c2:c2 + 1], in1=st[:, c3:c3 + 1], op=ALU.add), fragile=True)
        R.op("dve", lambda e: e.tensor_scalar(
            out=st[:, c4:c4 + 1], in0=st[:, c4:c4 + 1], scalar1=1e-9, scalar2=0.0,
            op0=ALU.add, op1=ALU.add), fragile=True)
        R.op("dve", lambda e: e.reciprocal(out=st[:, c4:c4 + 1], in_=st[:, c4:c4 + 1]), fragile=True)
        R.op("dve", lambda e: e.tensor_tensor(
            out=st[:, c2:c2 + 1], in0=st[:, c2:c2 + 1], in1=st[:, c4:c4 + 1], op=ALU.mult), fragile=True)
        R.op("dve", lambda e: e.tensor_tensor(
            out=st[:, c3:c3 + 1], in0=st[:, c3:c3 + 1], in1=st[:, c4:c4 + 1], op=ALU.mult), fragile=True)
        R.op("dve", lambda e: e.tensor_scalar(
            out=msd, in0=msd, scalar1=st[:, c3:c3 + 1], scalar2=0.0,
            op0=ALU.mult, op1=ALU.add), fragile=True)
        go = (blk % 2) * 32 + tl * 8
        dG = R.op("dve", lambda e, go=go: e.scalar_tensor_tensor(
            out=self.G_blk[:, go:go + 8], in0=msk, scalar=st[:, c2:c2 + 1],
            in1=msd, op0=ALU.mult, op1=ALU.add), inc="dve", fragile=True)
        self.dep_G = dG

    def emit_expert(self, idx, blk, ex, xn_off, h_off, dep_xn):
        R = self.rec
        b1c = self.po("b1") + 32 * ex

        def gelu_evac(fc, bank, mm_dep):
            R.wait("act", mm_dep)
            R.wait("act", self.h_rd[fc])
            dd = R.op("act", lambda e, fc=fc, bank=bank: e.activation(
                out=self.abf(h_off + fc * 512, 512), in_=self.pb[bank][:],
                func=AF.Gelu_apprx_tanh,
                bias=self.pfm[:, b1c + fc:b1c + fc + 1]), inc="act")
            self.h_dep[fc] = dd
            return dd
        self.h_dep = getattr(self, "h_dep", [None] * 32)
        self.h_rd = getattr(self, "h_rd", [None] * 32)
        R.wait("pe", dep_xn)
        self.modeA(8, 32, 2,
                   stat_tile=lambda u_off, kc, mc: u_off + (((mc // 2) % 4) * 16 + kc * 2 + mc % 2) * 128,
                   moving=lambda kc: self.abf(xn_off + kc * 512, 512),
                   evac=gelu_evac,
                   ring_units=4,
                   unit_src=lambda u, ex=ex: self.inp["W1_units"][ex][u],
                   tiles_per_unit_grp=4)

        def w2_evac(tl, dh, bank, dep):
            t0 = blk * 4 + tl
            xt = self.xs[:, t0 * 1024 + dh * 512: t0 * 1024 + dh * 512 + 512]
            R.wait("dve", dep)
            R.wait("dve", self.dep_G)
            go = (blk % 2) * 32 + tl * 8 + ex
            dd = R.op("dve", lambda e, xt=xt, bank=bank, go=go: e.scalar_tensor_tensor(
                out=xt, in0=self.pb[bank][:], scalar=self.G_blk[:, go:go + 1],
                in1=xt, op0=ALU.mult, op1=ALU.add), inc="dve")
            self.stream_dep[t0] = dd
            return dd

        def h_stat(kc, tl):
            R.wait("pe", self.h_dep[kc])
            return self.abf(h_off + kc * 512 + tl * 128, 128)
        self.modeB(32, stat=h_stat,
                   moving=lambda u_off, kc, dh: self.ring[:, u_off + (kc % 8) * 1024 + dh * 512:
                                                         u_off + (kc % 8) * 1024 + dh * 512 + 512],
                   t0locs=list(range(4)),
                   bias_cols=(2048 + ex * 1024, 2048 + ex * 1024 + 512),
                   evac=w2_evac, ring_units=4,
                   unit_src=lambda u, ex=ex: self.inp["W2_units"][ex][u], k_per_unit=8)
        for fc in range(32):
            self.h_rd[fc] = R.now("pe")


    def emit_attn_b(self):
        R = self.rec
        A = self.arena
        A.push()
        U = A.alloc(7168)
        z1_off, sq_off, jnk_off = U, U + 2048, U + 4096
        sc = {"uc": U, "sd": U + 2048, "sg": U + 4096, "ar": U + 5120, "vs": U + 6144}
        zb_off = A.alloc(1024)
        zg_off = A.alloc(4096)
        xc_off = A.alloc(8192)
        xg_off = A.alloc(8192)
        xp_off = A.alloc(2 * 1032)
        ca_off = A.alloc(2 * 1024)
        self.zb_rd = None
        self.xc_dep = [None] * 16
        self.xg_dep = [None] * 16
        self.xp_rd = [None, None]
        self.ca_rd = [None, None]
        self.sd_rd = [None, None]
        self.sg_rd = None

        for blk in range(NBLK):
            dep_zg = self.ln_block(blk, z1_off, sq_off, jnk_off, zb_off, zg_off,
                                   self.po("ln_g"), self.po("ln_b"),
                                   fold=True, double=True)
            for ecg in range(8):
                self.emit_winxp_pair(blk, ecg, zg_off, xc_off, xp_off, ca_off, dep_zg)
            for p in range(8):
                self.emit_ssm_pair(blk, p, zg_off, xc_off, xg_off, sc)
            def o_evac(tl, dh, bank, dep):
                t0 = blk * 4 + tl
                xt = self.xs[:, t0 * 1024 + dh * 512: t0 * 1024 + dh * 512 + 512]
                R.wait("dve", dep)
                dd = R.op("dve", lambda e, xt=xt, bank=bank: e.tensor_tensor(
                    out=xt, in0=xt, in1=self.pb[bank][:], op=ALU.add), inc="dve")
                self.stream_dep[t0] = dd
                return dd

            def xg_stat(kc, tl):
                R.wait("pe", self.xg_dep[kc])
                return self.abf(xg_off + kc * 512 + tl * 128, 128)
            self.modeB(16, stat=xg_stat,
                       moving=lambda u_off, kc, dh: self.ring[:, u_off + kc * 1024 + dh * 512:
                                                             u_off + kc * 1024 + dh * 512 + 512],
                       t0locs=list(range(4)), bias_cols=(1024, 1536),
                       evac=o_evac, ring_units=1,
                       unit_src=lambda u: self.inp["WoT_pack"][0], k_per_unit=16)
            self.barrier()
        A.pop()

    def emit_winxp_pair(self, blk, ecg, zg_off, xc_off, xp_off, ca_off, dep_zg):
        R = self.rec
        banks = [0, 1] if ecg % 2 == 0 else [2, 3]
        for b in banks:
            self.wait_bank("pe", b)
        off, udep, slot = self.ring_load(self.inp["WinT_xp"][ecg], 2048)
        R.wait("pe", udep)
        R.wait("pe", dep_zg)
        mm = None
        for kc in range(8):
            for i in range(2):
                t = off + (kc * 2 + i) * 128
                mm = R.op("pe", lambda e, t=t, b=banks[i], kc=kc: e.matmul(
                    out=self.pb[b][:], lhsT=self.ring[:, t:t + 128],
                    rhs=self.abf(zg_off + kc * 512, 512),
                    start=(kc == 0), stop=(kc == 7)),
                    inc="pe" if (kc == 7 and i == 1) else None)
        self.ring_dep[slot] = mm
        cw = self.po("convw")
        for i in range(2):
            ec = 2 * ecg + i
            rot = ec % 2
            xp = self.af32(xp_off + rot * 1032, 516)
            ca = self.af32(ca_off + rot * 1024, 512)
            R.wait("act", mm)
            R.wait("act", self.xp_rd[rot])
            d0 = R.op("act", lambda e, i=i, ec=ec, xp=xp: e.activation(
                out=xp[:, 3:515], in_=self.pb[banks[i]][:], func=AF.Identity,
                bias=self.pfm[:, self.po("b_in") + ec:self.po("b_in") + ec + 1]),
                inc="act")
            self.use_bank(banks[i], d0)
            R.wait("dve", d0)
            R.wait("dve", self.dep_const)
            R.wait("dve", self.ca_rd[rot])
            if blk == 0:
                R.op("dve", lambda e, xp=xp: e.memset(xp[:, 0:3], 0.0))
            else:
                R.op("dve", lambda e, xp=xp, ec=ec: e.tensor_copy(
                    out=xp[:, 0:3], in_=self.xptail[:, ec * 3:ec * 3 + 3]), fragile=True)
            R.op("dve", lambda e, xp=xp, ca=ca, ec=ec, cw=cw: e.tensor_scalar(
                out=ca, in0=xp[:, 0:512], scalar1=self.pfm[:, cw + ec:cw + ec + 1],
                scalar2=0.0, op0=ALU.mult, op1=ALU.add))
            for j in range(1, 4):
                R.op("dve", lambda e, xp=xp, ca=ca, ec=ec, j=j, cw=cw: e.scalar_tensor_tensor(
                    out=ca, in0=xp[:, j:j + 512],
                    scalar=self.pfm[:, cw + 16 * j + ec:cw + 16 * j + ec + 1],
                    in1=ca, op0=ALU.mult, op1=ALU.add))
            dtl = R.op("dve", lambda e, xp=xp, ec=ec: e.tensor_copy(
                out=self.xptail[:, ec * 3:ec * 3 + 3], in_=xp[:, 512:515]), inc="dve")
            self.xp_rd[rot] = dtl
            R.wait("act", dtl)
            dsl = R.op("act", lambda e, ca=ca, ec=ec: e.activation(
                out=self.abf(xc_off + ec * 512, 512), in_=ca, func=AF.Silu,
                bias=self.pfm[:, self.po("conv_b") + ec:self.po("conv_b") + ec + 1]),
                inc="act")
            self.xc_dep[ec] = dsl
            self.ca_rd[rot] = dsl

    def emit_ssm_pair(self, blk, p, zg_off, xc_off, xg_off, sc):
        R = self.rec
        bC = [0, 1] if p % 2 == 0 else [4, 5]
        bD = [2, 3] if p % 2 == 0 else [6, 7]
        for kc in range(16):
            R.wait("pe", self.xc_dep[kc])

        def grp(units, banks, n_k, mov_off):
            for b in banks:
                self.wait_bank("pe", b)
            off, udep, slot = self.ring_load(units, None)
            R.wait("pe", udep)
            mm = None
            for kc in range(n_k):
                for i in range(2):
                    t = off + (kc * 2 + i) * 128
                    mm = R.op("pe", lambda e, t=t, b=banks[i], kc=kc, mo=mov_off, nk=n_k:
                              e.matmul(out=self.pb[b][:], lhsT=self.ring[:, t:t + 128],
                                       rhs=self.abf(mo + kc * 512, 512),
                                       start=(kc == 0), stop=(kc == nk - 1)),
                              inc="pe" if (kc == n_k - 1 and i == 1) else None)
            self.ring_dep[slot] = mm
            return mm

        mmC = grp(self.inp["W_cT_pack"][p], bC, 16, xc_off)
        mmD = grp(self.inp["W_dT_pack"][p], bD, 16, xc_off)
        for i in range(2):
            fc = 2 * p + i
            uc = self.af32(sc["uc"] + i * 1024, 512)
            R.wait("dve", mmC)
            R.wait("dve", self.z1_act_rd)
            duc = R.op("dve", lambda e, uc=uc, fc=fc, i=i: e.tensor_scalar(
                out=uc, in0=self.pb[bC[i]][:],
                scalar1=self.pfm[:, self.po("b_c") + fc:self.po("b_c") + fc + 1],
                scalar2=0.0, op0=ALU.add, op1=ALU.add), inc="dve")
            self.use_bank(bC[i], duc)
        mmG = grp(self.inp["WinT_gate"][p], bC, 8, zg_off)
        dsds = []
        for i in range(2):
            fc = 2 * p + i
            sd = self.af32(sc["sd"] + i * 1024, 512)
            R.wait("act", mmD)
            R.wait("act", self.sd_rd[i])
            dsd = R.op("act", lambda e, sd=sd, fc=fc, i=i: e.activation(
                out=sd, in_=self.pb[bD[i]][:], func=AF.Sigmoid,
                bias=self.pfm[:, self.po("b_d") + fc:self.po("b_d") + fc + 1]), inc="act")
            self.use_bank(bD[i], dsd)
            dsds.append(dsd)
        for i in range(2):
            fc = 2 * p + i
            uc = self.af32(sc["uc"] + i * 1024, 512)
            sd = self.af32(sc["sd"] + i * 1024, 512)
            sg = self.af32(sc["sg"], 512)
            ar = self.af32(sc["ar"], 512)
            vs = self.af32(sc["vs"], 512)
            R.wait("act", mmG)
            R.wait("act", self.sg_rd)
            dsg = R.op("act", lambda e, sg=sg, fc=fc, i=i: e.activation(
                out=sg, in_=self.pb[bC[i]][:], func=AF.Sigmoid,
                bias=self.pfm[:, self.po("b_in_g") + fc:self.po("b_in_g") + fc + 1]),
                inc="act")
            self.use_bank(bC[i], dsg)
            R.wait("dve", dsds[i])
            R.wait("dve", dsg)
            R.op("dve", lambda e, uc=uc, sd=sd: e.tensor_tensor(
                out=uc, in0=uc, in1=sd, op=ALU.mult))
            R.op("dve", lambda e, ar=ar: e.memset(ar, 1.0))
            R.op("dve", lambda e, ar=ar, fc=fc: e.tensor_scalar(
                out=ar, in0=ar, scalar1=self.pext[:, fc:fc + 1], scalar2=0.0,
                op0=ALU.mult, op1=ALU.add))
            R.op("dve", lambda e, ar=ar, uc=uc, vs=vs, fc=fc: e.tensor_tensor_scan(
                out=vs, data0=ar, data1=uc, initial=self.vcar[:, fc:fc + 1],
                op0=ALU.mult, op1=ALU.add), fragile=True)
            R.op("dve", lambda e, vs=vs, fc=fc: e.tensor_copy(
                out=self.vcar[:, fc:fc + 1], in_=vs[:, 511:512]), fragile=True)
            dxg = R.op("dve", lambda e, vs=vs, sg=sg, fc=fc: e.scalar_tensor_tensor(
                out=self.abf(xg_off + fc * 512, 512), in0=vs,
                scalar=self.pext[:, 16 + fc:16 + fc + 1], in1=sg,
                op0=ALU.mult, op1=ALU.mult), inc="dve")
            self.xg_dep[fc] = dxg
            self.sg_rd = dxg
            self.sd_rd[i] = dxg

    def emit_store(self):
        R = self.rec
        for t0 in range(NT):
            R.wait("sp", self.stream_dep[t0])
        outr = self.out_hbm.rearrange("(t p) d -> p t d", p=128)
        xsv = self.xs[:].rearrange("p (t d) -> p t d", t=NT)
        dst = R.op("sp", lambda e: e.dma_start(out=outr, in_=xsv), inc="st", dma=True)
        if self.dbg is not None:
            off, n = self.dbg
            dbg_hbm = self.nc.dram_tensor("dbg", [128, n], BF16,
                                          kind="ExternalOutput").ap()
            for e_ in ("pe", "act", "dve"):
                R.wait("sp", R.now(e_))
            dst = R.op("sp", lambda e: e.dma_start(out=dbg_hbm, in_=self.abf(off, n)),
                       inc="st", dma=True)
        if getattr(self, "dbg_t", None) is not None:
            nm, n = self.dbg_t
            t = getattr(self, nm)
            dbg2_hbm = self.nc.dram_tensor("dbg2", [128, n], F32,
                                           kind="ExternalOutput").ap()
            for e_ in ("pe", "act", "dve"):
                R.wait("sp", R.now(e_))
            dst = R.op("sp", lambda e: e.dma_start(out=dbg2_hbm, in_=t[:, 0:n]),
                       inc="st", dma=True)
        R.wait("sp", dst)



_CACHE = {}


def kernel(**inputs):
    x = np.asarray(inputs["L0_attn_a_x"], np.float32)
    packed = prepare_inputs(inputs)
    key = _UPTO
    if key not in _CACHE:
        _CACHE[key] = Kern(upto=_UPTO).build()
    nc = _CACHE[key]
    common = {k: v for k, v in packed.items() if k != "pfm_layout"}
    in_maps = [dict(common, x=np.ascontiguousarray(x[i])) for i in range(8)]
    res = run_bass_kernel_spmd(nc, in_maps, core_ids=list(range(8)))
    return np.stack([r["out"] for r in res.results]).astype(np.float32)


# version 04
# speedup vs baseline: 1.0777x; 1.0777x over previous
import numpy as np
import ml_dtypes
import concourse.bass as bass
import concourse.mybir as mybir
from concourse.bass_utils import run_bass_kernel_spmd
from contextlib import ExitStack

F32 = mybir.dt.float32
BF16 = mybir.dt.bfloat16
I32 = mybir.dt.int32
AF = mybir.ActivationFunctionType
ALU = mybir.AluOpType
AX = mybir.AxisListType
BF = ml_dtypes.bfloat16

T, D = 2048, 1024
E2, NS = 2048, 16
FF, NEXP = 4096, 8
H, DH = 16, 64
NT = T // 128
TBLK = 512
NBLK = T // TBLK
EPS_LN = 1e-5
EPS_ATT = 1e-6

ENGINES = ("pe", "act", "dve", "pool", "sp")
RING_SLOTS = 3
RING_COLS = 8192
ARENA_COLS = 32896

_UPTO = "all"


def _pfm_layout():
    lay, off = {}, 0
    for name, n in [("gamma", 8), ("beta", 8), ("ln_g", 8), ("ln_b", 8),
                    ("b1", 256), ("b_in", 16), ("b_in_g", 16), ("conv_b", 16),
                    ("b_c", 16), ("b_d", 16), ("logd", 16), ("convw", 64),
                    ("A", 256), ("Bm", 256)]:
        lay[name] = off
        off += n
    return lay, off


PFM_LAYOUT, PFM_COLS = _pfm_layout()


def ceil_div(a, b):
    return (a + b - 1) // b



def pack_tiles(WT, order):
    K, M = WT.shape
    W4 = WT.reshape(K // 128, 128, M // 128, 128)
    return np.ascontiguousarray(np.stack([W4[kc, :, mc, :] for kc, mc in order]))


def modeA_order(n_k, n_m, grp):
    order = []
    for mg in range(ceil_div(n_m, grp)):
        for kc in range(n_k):
            for mc in range(mg * grp, min((mg + 1) * grp, n_m)):
                order.append((kc, mc))
    return order


def tiles_to_units(tiles, tiles_per_unit):
    n = tiles.shape[0] // tiles_per_unit
    return np.ascontiguousarray(
        tiles.reshape(n, tiles_per_unit, 128, 128).transpose(0, 2, 1, 3)
        .reshape(n, 128, tiles_per_unit * 128))


def rows_pack(WT):
    K, M = WT.shape
    return np.ascontiguousarray(
        WT.reshape(K // 128, 128, M).transpose(1, 0, 2)).reshape(1, 128, (K // 128) * M)


def fmv(v, nchunk):
    return np.ascontiguousarray(np.asarray(v, np.float32).reshape(nchunk, 128).T)


def prepare_inputs(inp):
    f32 = np.float32
    tp = lambda a: np.ascontiguousarray(np.asarray(a, f32).T)
    d = {}
    for nm in ("Wq", "Wk"):
        d[f"{nm}T_pack"] = tiles_to_units(
            pack_tiles(tp(inp[f"L0_attn_a_{nm}"]), modeA_order(8, 8, 4)), 64)
    for nm in ("Wv", "Wp"):
        d[f"{nm}T_pack"] = rows_pack(tp(inp[f"L0_attn_a_{nm}"]))
    W1 = np.asarray(inp["L0_ffn_W1"], f32)
    W2 = np.asarray(inp["L0_ffn_W2"], f32)
    d["W1_units"] = np.stack([tiles_to_units(
        pack_tiles(W1[e], modeA_order(8, 32, 2)), 64) for e in range(NEXP)])
    d["W2_units"] = np.stack([np.ascontiguousarray(
        W2[e].reshape(4, 8, 128, 1024).transpose(0, 2, 1, 3).reshape(4, 128, 8192))
        for e in range(NEXP)])
    WgT = tp(inp["L0_ffn_Wg"])
    d["WgT_fm"] = np.ascontiguousarray(
        WgT.reshape(8, 128, 8).transpose(1, 0, 2)).reshape(128, 64)
    WinT = tp(inp["L0_attn_b_W_in"])
    d["WinT_xp"] = tiles_to_units(pack_tiles(WinT[:, :E2], modeA_order(8, 16, 2)), 16)
    d["WinT_gate"] = tiles_to_units(pack_tiles(WinT[:, E2:], modeA_order(8, 16, 2)), 16)
    for nm in ("W_c", "W_d"):
        d[f"{nm}T_pack"] = tiles_to_units(
            pack_tiles(tp(inp[f"L0_attn_b_{nm}"]), modeA_order(16, 16, 2)), 32)
    d["WoT_pack"] = rows_pack(tp(inp["L0_attn_b_W_out"]))

    cols = []
    for name, arr in [
        ("gamma", fmv(inp["L0_ffn_gamma"], 8)), ("beta", fmv(inp["L0_ffn_beta"], 8)),
        ("ln_g", fmv(inp["L0_attn_b_ln_g"], 8)), ("ln_b", fmv(inp["L0_attn_b_ln_b"], 8)),
        ("b1", np.concatenate([fmv(np.asarray(inp["L0_ffn_b1"], f32)[e], 32)
                               for e in range(NEXP)], axis=1)),
        ("b_in", fmv(np.asarray(inp["L0_attn_b_b_in"], f32)[:E2], 16)),
        ("b_in_g", fmv(np.asarray(inp["L0_attn_b_b_in"], f32)[E2:], 16)),
        ("conv_b", fmv(inp["L0_attn_b_conv_b"], 16)),
        ("b_c", fmv(inp["L0_attn_b_b_c"], 16)),
        ("b_d", fmv(inp["L0_attn_b_b_d"], 16)),
        ("logd", fmv(inp["L0_attn_b_log_delta"], 16)),
        ("convw", np.concatenate([fmv(np.ascontiguousarray(
            np.asarray(inp["L0_attn_b_conv_w"], f32)[:, 0, j]), 16) for j in range(4)], axis=1)),
        ("A", np.concatenate([fmv(np.ascontiguousarray(
            np.asarray(inp["L0_attn_b_A"], f32)[:, n]), 16) for n in range(NS)], axis=1)),
        ("Bm", np.concatenate([fmv(np.ascontiguousarray(
            np.asarray(inp["L0_attn_b_Bm"], f32)[:, n]), 16) for n in range(NS)], axis=1)),
    ]:
        assert PFM_LAYOUT[name] == sum(c.shape[1] for c in cols)
        cols.append(arr)
    d["pfm_h"] = np.ascontiguousarray(np.concatenate(cols, axis=1))
    assert d["pfm_h"].shape[1] == PFM_COLS
    rows = np.concatenate([np.asarray(inp["L0_attn_a_bp"], f32),
                           np.asarray(inp["L0_attn_b_b_out"], f32),
                           np.asarray(inp["L0_ffn_b2"], f32).reshape(-1)])
    d["rows_h"] = np.ascontiguousarray(rows.reshape(1, -1))
    return d



class Rec:
    def __init__(self):
        self.thunks = {k: [] for k in ENGINES}
        self.counts = {}
        self.handles = {}
        self.last_wait = {k: {} for k in ENGINES}
        self.bank_dep = [None] * 8

    def op(self, eng, build, inc=None, dma=False, fragile=False):
        if fragile:
            self.wait(eng, self.now(eng))
            inc = inc or eng
        if inc is None:
            self.thunks[eng].append(build)
            return None
        amt = 16 if dma else 1
        self.counts[inc] = self.counts.get(inc, 0) + amt
        tgt = self.counts[inc]

        def fn(e, b=build, s=inc, a=amt):
            b(e).then_inc(self.handles[s], a)
        self.thunks[eng].append(fn)
        return (inc, tgt)

    def wait(self, eng, dep):
        if dep is None:
            return
        s, v = dep
        if v <= self.last_wait[eng].get(s, 0):
            return
        self.last_wait[eng][s] = v
        self.thunks[eng].append(lambda e, s=s, v=v: e.wait_ge(self.handles[s], v))

    def now(self, sem):
        return (sem, self.counts.get(sem, 0))

    def sem_names(self):
        names = set(self.counts.keys())
        names.update(["pe", "act", "dve", "pool", "ld", "in", "st"])
        return sorted(names)


class Arena:
    def __init__(self, cols):
        self.cols = cols
        self.off = 0
        self.marks = []

    def push(self):
        self.marks.append(self.off)

    def pop(self):
        self.off = self.marks.pop()

    def alloc(self, n):
        n = (n + 1) & ~1
        assert self.off + n <= self.cols, f"arena overflow {self.off}+{n}"
        o = self.off
        self.off += n
        return o



class Kern:
    ln_cast_dep = None

    def __init__(self, upto="all", sub=99, core_n=9999, core_p=99):
        self.upto = upto
        self.sub = sub
        self.core_n = core_n
        self.core_p = core_p
        self.core_mask = True
        self.dbg = None
        self.nc = bass.Bass("TRN2", target_bir_lowering=False,
                            detect_race_conditions=False)
        self.rec = Rec()
        self.stack = ExitStack()
        self.ring_turn = 0
        self.ring_dep = [None] * RING_SLOTS
        self.tr_turn = 0
        self.stream_dep = [None] * NT
        self.z1_act_rd = None

    def po(self, name):
        return PFM_LAYOUT[name]

    def abf(self, off, n):
        return self.arena_t[:, off:off + n]

    def af32(self, off, n):
        return self.arena_t[:].bitcast(F32)[:, off // 2:off // 2 + n]

    def bank_bf(self, i):
        return self.pb[i][:].bitcast(BF16)

    def wait_bank(self, eng, i):
        self.rec.wait(eng, self.rec.bank_dep[i])

    def use_bank(self, i, dep):
        self.rec.bank_dep[i] = dep

    def ring_load(self, src_ap, n_cols=None):
        R = self.rec
        if n_cols is None:
            n_cols = src_ap.shape[-1]
        slot = self.ring_turn % RING_SLOTS
        self.ring_turn += 1
        off = slot * RING_COLS
        R.wait("pool", self.ring_dep[slot])
        dep = R.op("pool", lambda e, s=src_ap, o=off, n=n_cols:
                   e.dma_start(out=self.ring[:, o:o + n], in_=s), inc="ld", dma=True)
        return off, dep, slot

    def barrier(self):
        R = self.rec
        deps = {e: R.now(e) for e in ("pe", "act", "dve")}
        for e in ("pe", "act", "dve"):
            for o in ("pe", "act", "dve"):
                if o != e:
                    R.wait(e, deps[o])


    def modeA(self, n_k, n_m, grp, stat_tile, moving, evac, ring_units, unit_src,
              tiles_per_unit_grp):
        R = self.rec
        n_grp = ceil_div(n_m, grp)
        grp_per_unit = tiles_per_unit_grp
        u_off = u_dep = u_slot = None
        last_pe_for_slot = None
        for mg in range(n_grp):
            if mg % grp_per_unit == 0:
                if u_slot is not None:
                    self.ring_dep[u_slot] = last_pe_for_slot
                u = mg // grp_per_unit
                u_off, u_dep, u_slot = self.ring_load(unit_src(u), None)
            mcs = list(range(mg * grp, min((mg + 1) * grp, n_m)))
            base = 0 if (mg % 2 == 0) else 4
            banks = [base + i for i in range(len(mcs))]
            for b in banks:
                self.wait_bank("pe", b)
            R.wait("pe", u_dep)
            mm = None
            for kc in range(n_k):
                for i, mc in enumerate(mcs):
                    t = stat_tile(u_off, kc, mc)
                    mv = moving(kc)
                    mm = R.op("pe", lambda e, t=t, b=banks[i], mv=mv, s=(kc == 0), p=(kc == n_k - 1):
                              e.matmul(out=self.pb[b][:], lhsT=self.ring[:, t:t + 128],
                                       rhs=mv, start=s, stop=p),
                              inc="pe" if (kc == n_k - 1 and i == len(mcs) - 1) else None)
            last_pe_for_slot = mm
            for i, mc in enumerate(mcs):
                dep = evac(mc, banks[i], mm)
                self.use_bank(banks[i], dep)
        if u_slot is not None:
            self.ring_dep[u_slot] = R.now("pe")

    def modeB(self, n_k, stat, moving, t0locs, bias_cols, evac, ring_units, unit_src,
              k_per_unit):
        R = self.rec
        for b in range(2 * len(t0locs)):
            self.wait_bank("pe", b)
        has_bias = bias_cols is not None
        u_off = u_dep = u_slot = None
        mmlast = None
        for kc in range(n_k):
            if kc % k_per_unit == 0:
                if u_slot is not None:
                    self.ring_dep[u_slot] = mmlast
                u = kc // k_per_unit
                u_off, u_dep, u_slot = self.ring_load(unit_src(u), None)
                R.wait("pe", u_dep)
            unit_end = (kc % k_per_unit == k_per_unit - 1) or kc == n_k - 1
            for t0i, tl in enumerate(t0locs):
                s = stat(kc, tl)
                for dh in range(2):
                    mv = moving(u_off, kc, dh)
                    final = (kc == n_k - 1) and not has_bias
                    mark = unit_end and t0i == len(t0locs) - 1 and dh == 1
                    mmlast = R.op("pe", lambda e, s=s, mv=mv, b=2 * t0i + dh,
                                  st=(kc == 0), sp=final:
                                  e.matmul(out=self.pb[b][:], lhsT=s, rhs=mv,
                                           start=st, stop=sp),
                                  inc="pe" if mark else None)
        if u_slot is not None:
            self.ring_dep[u_slot] = mmlast
        if has_bias:
            R.wait("pe", self.dep_rows)
            for t0i in range(len(t0locs)):
                for dh in range(2):
                    ro = bias_cols[dh]
                    mmlast = R.op("pe", lambda e, ro=ro, b=2 * t0i + dh: e.matmul(
                        out=self.pb[b][:], lhsT=self.ones1r[:],
                        rhs=self.rows_bf[0:1, ro:ro + 512], start=False, stop=True),
                        inc="pe" if (t0i == len(t0locs) - 1 and dh == 1) else None)
        for t0i, tl in enumerate(t0locs):
            for dh in range(2):
                dep = evac(tl, dh, 2 * t0i + dh, mmlast)
                self.use_bank(2 * t0i + dh, dep)


    def build(self):
        nc, R = self.nc, self.rec
        E = self.stack.enter_context
        self.inp = {}
        self.x_hbm = nc.dram_tensor("x", [T, D], F32, kind="ExternalInput").ap()
        self.out_hbm = nc.dram_tensor("out", [T, D], F32, kind="ExternalOutput").ap()
        for nm, shape in [
            ("WqT_pack", (1, 128, 8192)), ("WkT_pack", (1, 128, 8192)),
            ("WvT_pack", (1, 128, 8192)), ("WpT_pack", (1, 128, 8192)),
            ("W1_units", (8, 4, 128, 8192)), ("W2_units", (8, 4, 128, 8192)),
            ("WgT_fm", (128, 64)),
            ("WinT_xp", (8, 128, 2048)), ("WinT_gate", (8, 128, 2048)),
            ("W_cT_pack", (8, 128, 4096)), ("W_dT_pack", (8, 128, 4096)),
            ("WoT_pack", (1, 128, 16384)),
            ("pfm_h", (128, PFM_COLS)), ("rows_h", (1, 10240)),
        ]:
            self.inp[nm] = nc.dram_tensor(nm, list(shape), F32, kind="ExternalInput").ap()

        self.xs = E(nc.sbuf_tensor("xs", [128, NT * 1024], F32))
        self.ident = E(nc.sbuf_tensor("ident", [128, 128], BF16))
        self.mask4 = E(nc.sbuf_tensor("mask4", [128, 512], BF16))
        self.iota_i = E(nc.sbuf_tensor("iota_i", [128, 128], I32))
        self.iota_f = E(nc.sbuf_tensor("iota_f", [128, 128], F32))
        self.ones1r = E(nc.sbuf_tensor("ones1r", [1, 128], BF16))
        self.rows_bf = E(nc.sbuf_tensor("rows_bf", [1, 10240], BF16))
        self.pfm = E(nc.sbuf_tensor("pfm", [128, PFM_COLS], F32))
        self.pext = E(nc.sbuf_tensor("pext", [128, 32], F32))
        self.wg_bf = E(nc.sbuf_tensor("wg_bf", [128, 64], BF16))
        self.S_all = E(nc.sbuf_tensor("S_all", [128, 520], F32))
        self.S_bf = E(nc.sbuf_tensor("S_bf", [128, 1040], BF16))
        self.vcar = E(nc.sbuf_tensor("vcar", [128, 16], F32))
        self.xptail = E(nc.sbuf_tensor("xptail", [128, 48], F32))
        self.G_blk = E(nc.sbuf_tensor("G_blk", [128, 64], F32))
        self.stat = E(nc.sbuf_tensor("stat", [128, 96], F32))
        self.gsc = E(nc.sbuf_tensor("gsc", [128, 24], F32))
        self.ring = E(nc.sbuf_tensor("ring", [128, RING_SLOTS * RING_COLS], BF16))
        self.arena_t = E(nc.sbuf_tensor("arena", [128, ARENA_COLS], BF16))
        self.arena = Arena(ARENA_COLS)
        self.pb = [E(nc.psum_tensor(f"pb{i}", [128, 512], F32)) for i in range(8)]

        self.emit_init()
        self.emit_attn_a()
        if self.upto != "attn_a":
            self.emit_ffn(0)
            if self.upto not in ("ffn1",):
                self.emit_attn_b()
                if self.upto not in ("attn_b",):
                    self.emit_ffn(1)
        self.emit_store()

        for s in R.sem_names():
            R.handles[s] = E(nc.semaphore(s))
        blk = E(nc.Block())
        th = R.thunks

        @blk.tensor
        def _(e):
            for fn in th["pe"]:
                fn(e)

        @blk.scalar
        def _(e):
            for fn in th["act"]:
                fn(e)

        @blk.vector
        def _(e):
            for fn in th["dve"]:
                fn(e)

        @blk.gpsimd
        def _(e):
            for fn in th["pool"]:
                fn(e)

        @blk.sync
        def _(e):
            for fn in th["sp"]:
                fn(e)

        self.stack.close()
        return nc


    def emit_init(self):
        R = self.rec
        xr = self.x_hbm.rearrange("(t p) d -> p t d", p=128)
        xsv = self.xs[:].rearrange("p (t d) -> p t d", t=NT)
        self.dep_x = R.op("sp", lambda e: e.dma_start(out=xsv, in_=xr),
                          inc="in", dma=True)
        d1 = R.op("sp", lambda e: e.dma_start(out=self.pfm[:], in_=self.inp["pfm_h"]),
                  inc="in", dma=True)
        self.dep_pfm = d1
        self.dep_rows = R.op("pool", lambda e: e.dma_start(
            out=self.rows_bf[:], in_=self.inp["rows_h"]), inc="ld", dma=True)
        self.dep_wg = R.op("pool", lambda e: e.dma_start(
            out=self.wg_bf[:], in_=self.inp["WgT_fm"]), inc="ld", dma=True)
        di = R.op("pool", lambda e: e.iota(self.iota_i[:], pattern=[[1, 128]], base=0,
                                           channel_multiplier=-1), inc="pool")
        R.wait("dve", di)
        R.op("dve", lambda e: e.tensor_copy(out=self.iota_f[:], in_=self.iota_i[:]))
        R.op("dve", lambda e: e.tensor_single_scalar(
            out=self.ident[:], in_=self.iota_f[:], scalar=0.0, op=ALU.is_equal))
        for j in range(4):
            R.op("dve", lambda e, j=j: e.tensor_single_scalar(
                out=self.mask4[:, 128 * j:128 * j + 128], in_=self.iota_f[:],
                scalar=0.0, op=ALU.is_ge))
        R.op("dve", lambda e: e.memset(self.ones1r[:], 1.0))
        R.op("dve", lambda e: e.memset(self.stat[:], 0.0))
        R.op("dve", lambda e: e.memset(self.gsc[:], 0.0))
        R.op("dve", lambda e: e.memset(self.S_all[:], 0.0))
        R.op("dve", lambda e: e.memset(self.S_bf[:], 0.0))
        R.op("dve", lambda e: e.memset(self.vcar[:], 0.0))
        R.op("dve", lambda e: e.memset(self.xptail[:], 0.0))
        R.wait("act", d1)
        da = R.op("act", lambda e: e.activation(
            out=self.pext[:, 0:16], in_=self.pfm[:, self.po("logd"):self.po("logd") + 16],
            func=AF.Sigmoid, scale=-1.0), inc="act")
        R.wait("dve", da)
        R.wait("dve", d1)
        R.op("dve", lambda e: e.memset(self.pext[:, 16:32], 0.0))
        for n in range(NS):
            ao, bo = self.po("A") + 16 * n, self.po("Bm") + 16 * n
            R.op("dve", lambda e, ao=ao, bo=bo: e.tensor_tensor(
                out=self.stat[:, 0:16], in0=self.pfm[:, ao:ao + 16],
                in1=self.pfm[:, bo:bo + 16], op=ALU.mult), fragile=True)
            R.op("dve", lambda e: e.tensor_tensor(
                out=self.pext[:, 16:32], in0=self.pext[:, 16:32],
                in1=self.stat[:, 0:16], op=ALU.add), fragile=True)
        self.dep_const = R.op("dve", lambda e: e.nop(), inc="dve")


    def ln_tile(self, t0, z1_off, sq_off, jnk_off, sc, fold_residual, double):
        R = self.rec
        st = self.stat
        xt = self.xs[:, t0 * 1024:(t0 + 1) * 1024]
        z1 = self.af32(z1_off, 1024)
        sq = self.af32(sq_off, 1024)
        jnk = self.af32(jnk_off, 1024)

        def stats(src, cm, cr, src_dep_dve, src_dep_act):
            R.wait("dve", src_dep_dve)
            R.op("dve", lambda e: e.tensor_scalar(
                out=jnk, in0=src, scalar1=-1.0 / D, scalar2=0.0,
                op0=ALU.mult, op1=ALU.add, accum_out=st[:, cm:cm + 1]), fragile=True)
            R.wait("act", self.dep_const)
            R.wait("act", src_dep_act)
            db = R.op("act", lambda e: e.activation(
                out=sq, in_=src, func=AF.Square, scale=1.0 / 32.0,
                accum_out=st[:, cr + 1:cr + 2]), inc="act")
            R.wait("dve", db)
            R.op("dve", lambda e: e.tensor_tensor(
                out=st[:, cr:cr + 1], in0=st[:, cm:cm + 1], in1=st[:, cm:cm + 1],
                op=ALU.mult), fragile=True)
            R.op("dve", lambda e: e.tensor_scalar(
                out=st[:, cr:cr + 1], in0=st[:, cr + 1:cr + 2],
                scalar1=st[:, cr:cr + 1], scalar2=EPS_LN,
                op0=ALU.subtract, op1=ALU.add), fragile=True)
            dc = R.op("dve", lambda e: e.reciprocal(
                out=st[:, cr:cr + 1], in_=st[:, cr:cr + 1]), inc="dve", fragile=True)
            R.wait("act", dc)
            dd = R.op("act", lambda e: e.activation(
                out=st[:, cr:cr + 1], in_=st[:, cr:cr + 1], func=AF.Sqrt), inc="act")
            R.wait("dve", dd)
            return db

        sdep = self.stream_dep[t0] or self.dep_x
        db1 = stats(xt, sc, sc + 1, sdep, sdep)
        R.wait("dve", self.z1_act_rd)
        dz1 = R.op("dve", lambda e: e.tensor_scalar(
            out=z1, in0=xt, scalar1=st[:, sc:sc + 1], scalar2=st[:, sc + 1:sc + 2],
            op0=ALU.add, op1=ALU.mult), inc="dve", fragile=True)
        if fold_residual:
            R.wait("dve", db1)
            dres = R.op("dve", lambda e: e.tensor_tensor(
                out=xt, in0=xt, in1=z1, op=ALU.add), inc="dve")
            self.stream_dep[t0] = dres
        if double:
            db2 = stats(z1, sc + 3, sc + 4, None, dz1)
            self.z1_act_rd = db2
            R.wait("dve", db2)
            dz2 = R.op("dve", lambda e: e.tensor_scalar(
                out=z1, in0=z1, scalar1=st[:, sc + 3:sc + 4], scalar2=st[:, sc + 4:sc + 5],
                op0=ALU.add, op1=ALU.mult), inc="dve", fragile=True)
            return dz2
        return dz1

    def ln_transpose(self, t0loc, zb_off, dst_off, dst_stride, gcol, bcol, dep_zb):
        R = self.rec
        bk = 6 + (self.tr_turn % 2)
        self.tr_turn += 1
        self.wait_bank("pe", bk)
        R.wait("pe", dep_zb)
        dt = None
        for dc in range(8):
            dt = R.op("pe", lambda e, dc=dc, bk=bk: e.transpose(
                out=self.bank_bf(bk)[:, dc * 128:dc * 128 + 128],
                in_=self.abf(zb_off + 128 * dc, 128),
                identity=self.ident[:]), inc="pe" if dc == 7 else None)
        R.wait("dve", dt)
        dd = None
        for dc in range(8):
            do = dst_off + dc * dst_stride + 128 * t0loc
            if gcol is None:
                dd = R.op("dve", lambda e, do=do, bk=bk, dc=dc: e.tensor_copy(
                    out=self.abf(do, 128),
                    in_=self.bank_bf(bk)[:, dc * 128:dc * 128 + 128]),
                    inc="dve" if dc == 7 else None)
            else:
                dd = R.op("dve", lambda e, do=do, bk=bk, dc=dc: e.tensor_scalar(
                    out=self.abf(do, 128),
                    in0=self.bank_bf(bk)[:, dc * 128:dc * 128 + 128],
                    scalar1=self.pfm[:, gcol + dc:gcol + dc + 1],
                    scalar2=self.pfm[:, bcol + dc:bcol + dc + 1],
                    op0=ALU.mult, op1=ALU.add), inc="dve" if dc == 7 else None)
        self.use_bank(bk, dd)
        return dd

    def ln_block(self, blk, z1_off, sq_off, jnk_off, zb_off, dst_off, gcol, bcol,
                 fold, double):
        R = self.rec
        for tl in range(4):
            t0 = blk * 4 + tl
            dz = self.ln_tile(t0, z1_off, sq_off, jnk_off, 8 * (t0 % 8), fold, double)
            R.wait("dve", self.zb_rd)
            dzb = R.op("dve", lambda e: e.tensor_copy(
                out=self.abf(zb_off, 1024), in_=self.af32(z1_off, 1024)), inc="dve")
            self.ln_cast_dep = dzb
            dlast = self.ln_transpose(tl, zb_off, dst_off, 512, gcol, bcol, dzb)
            self.zb_rd = R.now("pe")
        return R.now("dve")

    def ln_stats_block(self, blk, z1_off, sq_off, jnk_off, zblk_off, fold, double):
        R = self.rec
        deps = []
        for tl in range(4):
            t0 = blk * 4 + tl
            dz = self.ln_tile(t0, z1_off, sq_off, jnk_off, 8 * (t0 % 8), fold, double)
            R.wait("dve", self.zblk_rd)
            dzb = R.op("dve", lambda e, tl=tl: e.tensor_copy(
                out=self.abf(zblk_off + 1024 * tl, 1024), in_=self.af32(z1_off, 1024)),
                inc="dve")
            self.ln_cast_dep = dzb
            deps.append(dzb)
        return deps

    def ln_tr_block(self, zblk_off, dst_off, gcol, bcol, dzbs):
        R = self.rec
        for tl in range(4):
            self.ln_transpose(tl, zblk_off + 1024 * tl, dst_off, 512, gcol, bcol,
                              dzbs[tl])
        self.zblk_rd = R.now("pe")
        return R.now("dve")


    def emit_attn_a(self):
        R = self.rec
        A = self.arena
        A.push()
        z1_off = A.alloc(2048)
        sq_off = A.alloc(2048)
        jnk_off = A.alloc(2048)
        zb_off = A.alloc(1024)
        y0_off = A.alloc(4096)
        pq_off = A.alloc(8192)
        pk_off = A.alloc(4096)
        v_off = A.alloc(4 * 1040)
        at_off = A.alloc(2048)
        pkt_off = A.alloc(2048)
        yt_off = A.alloc(1024)
        yf_off = y0_off
        self.zb_rd = None
        self.phi_prev = (None, None)
        ph1, ph2 = z1_off, sq_off
        R.wait("dve", self.dep_const)
        R.op("dve", lambda e: e.memset(self.abf(pq_off, 8192), 0.0))
        self.dep_sbf = R.op("dve", lambda e: e.nop(), inc="dve")

        for blk in range(NBLK):
            if self.sub < 1:
                break
            dep_y0 = self.ln_block(blk, z1_off, sq_off, jnk_off, zb_off, y0_off,
                                   None, None, fold=False, double=False)
            if self.sub < 2:
                continue

            for nm, dst in (("WqT_pack", pq_off), ("WkT_pack", pk_off)):
                def phi_evac(mc, bank, mm_dep, dst=dst, nm=nm):
                    pm = self.af32(ph1, 512)
                    pr = self.af32(ph2, 512)
                    pex = self.af32(jnk_off, 512)
                    prev_d2, prev_d3 = self.phi_prev
                    R.wait("act", mm_dep)
                    R.wait("act", prev_d3)
                    R.op("act", lambda e: e.activation(
                        out=pr, in_=self.pb[bank][:], func=AF.Relu))
                    d1 = R.op("act", lambda e: e.activation(
                        out=pm, in_=self.pb[bank][:], func=AF.Relu, scale=-1.0),
                        inc="act")
                    d2 = R.op("act", lambda e: e.activation(
                        out=pex, in_=pm, func=AF.Exp, scale=-1.0), inc="act")
                    R.wait("dve", d2)
                    if nm == "WqT_pack":
                        R.op("dve", lambda e, mc=mc: e.tensor_tensor(
                            out=self.abf(dst + (2 * mc) * 512, 512)[0:64, :],
                            in0=pex[0:64, :], in1=pr[0:64, :], op=ALU.add))
                        d3 = R.op("dve", lambda e, mc=mc: e.tensor_tensor(
                            out=self.abf(dst + (2 * mc + 1) * 512, 512)[64:128, :],
                            in0=pex[64:128, :], in1=pr[64:128, :], op=ALU.add),
                            inc="dve")
                    else:
                        d3 = R.op("dve", lambda e, mc=mc: e.tensor_tensor(
                            out=self.abf(dst + mc * 512, 512), in0=pex, in1=pr,
                            op=ALU.add), inc="dve")
                    self.phi_prev = (d2, d3)
                    return d1
                self.phi_prev = (None, None)
                R.wait("pe", dep_y0)
                self.modeA(8, 8, 4,
                           stat_tile=lambda u_off, kc, mc: u_off + (((mc // 4) * 8 + kc) * 4 + mc % 4) * 128,
                           moving=lambda kc: self.abf(y0_off + kc * 512, 512),
                           evac=phi_evac,
                           ring_units=1, unit_src=lambda u, nm=nm: self.inp[nm][0],
                           tiles_per_unit_grp=2)
            dep_phi = R.now("dve")
            if self.sub < 3:
                continue

            dms = R.op("dve", lambda e: e.memset(
                self.abf(v_off, 4160).rearrange("p (n h c) -> p n h c", n=4, h=16)[:, :, :, 64:65],
                1.0), inc="dve")

            def v_evac(tl, dh, bank, dep):
                R.wait("dve", dep)
                dstv = self.abf(v_off + tl * 1040, 1040).rearrange(
                    "p (h c) -> p h c", c=65)[:, 8 * dh:8 * dh + 8, 0:64]
                srcv = self.pb[bank][:].rearrange("p (h c) -> p h c", c=64)
                dd = R.op("dve", lambda e: e.tensor_copy(out=dstv, in_=srcv), inc="dve")
                return dd
            R.wait("pe", dep_y0)
            self.modeB(8, stat=lambda kc, tl: self.abf(y0_off + kc * 512 + tl * 128, 128),
                       moving=lambda u_off, kc, dh: self.ring[:, u_off + kc * 1024 + dh * 512:
                                                             u_off + kc * 1024 + dh * 512 + 512],
                       t0locs=list(range(4)), bias_cols=None,
                       evac=v_evac, ring_units=1,
                       unit_src=lambda u: self.inp["WvT_pack"][0], k_per_unit=8)
            dep_v = R.now("dve")
            if self.sub < 4:
                continue

            for nl in range(4):
                for w in range(2):
                    if blk * 8 + nl * 2 + w >= self.core_n:
                        continue
                    self.attn_core_wave(blk, nl, w, pq_off, pk_off, v_off,
                                        at_off, pkt_off, yt_off, yf_off)

            if self.sub < 5:
                continue
            def o_evac(tl, dh, bank, dep):
                t0 = blk * 4 + tl
                xt = self.xs[:, t0 * 1024 + dh * 512: t0 * 1024 + dh * 512 + 512]
                R.wait("dve", dep)
                dd = R.op("dve", lambda e, xt=xt, bank=bank: e.tensor_tensor(
                    out=xt, in0=xt, in1=self.pb[bank][:], op=ALU.add), inc="dve")
                self.stream_dep[t0] = dd
                return dd
            self.modeB(8, stat=lambda kc, tl: self.abf(yf_off + kc * 512 + tl * 128, 128),
                       moving=lambda u_off, kc, dh: self.ring[:, u_off + kc * 1024 + dh * 512:
                                                             u_off + kc * 1024 + dh * 512 + 512],
                       t0locs=list(range(4)), bias_cols=(0, 512),
                       evac=o_evac, ring_units=1,
                       unit_src=lambda u: self.inp["WpT_pack"][0], k_per_unit=8)
            self.barrier()
        A.pop()

    def attn_core_wave(self, blk, nl, w, pq_off, pk_off, v_off, at_off, pkt_off,
                       yt_off, yf_off):
        R = self.rec
        rot = (nl * 2 + w) % 2
        at = at_off + rot * 1024
        pkt = pkt_off + rot * 1024
        yt = yt_off + rot * 512
        bA = [0, 1] if rot == 0 else [2, 3]
        bKV = 4 if rot == 0 else 5
        bY = [0, 1] if rot == 1 else [2, 3]

        tb = 6 + (self.tr_turn % 2)
        self.tr_turn += 1
        self.wait_bank("pe", tb)
        dt = None
        for i in range(4):
            ec = 4 * w + i
            dt = R.op("pe", lambda e, ec=ec, i=i, tb=tb: e.transpose(
                out=self.bank_bf(tb)[:, i * 128:i * 128 + 128],
                in_=self.abf(pk_off + ec * 512 + nl * 128, 128),
                identity=self.ident[:]), inc="pe" if i == 3 else None)
        R.wait("act", dt)
        dpk = R.op("act", lambda e, tb=tb: e.activation(
            out=self.abf(pkt, 512), in_=self.bank_bf(tb)[:, 0:512],
            func=AF.Identity), inc="act")
        self.use_bank(tb, dpk)
        if self.core_p < 2:
            return

        for b in bA:
            self.wait_bank("pe", b)
        mmA = None
        for i in range(8):
            h = 8 * w + i
            ec = h // 2
            lk = self.abf(pk_off + ec * 512 + nl * 128, 128)
            rq = self.abf(pq_off + h * 512 + nl * 128, 128)
            mmA = R.op("pe", lambda e, lk=lk, rq=rq, b=bA[i // 4], c=(i % 4) * 128:
                       e.matmul(out=self.pb[b][:, c:c + 128], lhsT=lk, rhs=rq,
                                start=True, stop=True),
                       inc="pe" if i == 7 else None)
        R.wait("dve", mmA)
        dat = None
        for j in range(2):
            if self.core_mask:
                dat = R.op("dve", lambda e, j=j: e.tensor_tensor(
                    out=self.abf(at + j * 512, 512), in0=self.pb[bA[j]][:],
                    in1=self.mask4[:], op=ALU.mult), inc="dve")
            else:
                dat = R.op("dve", lambda e, j=j: e.tensor_copy(
                    out=self.abf(at + j * 512, 512), in_=self.pb[bA[j]][:]), inc="dve")
        for b in bA:
            self.use_bank(b, dat)
        if self.core_p < 3:
            return

        self.wait_bank("pe", bKV)
        R.wait("pe", dpk)
        mmKV = None
        for i in range(8):
            h = 8 * w + i
            lh = self.abf(pkt + i * 64, 64)
            rv = self.abf(v_off + nl * 1040 + h * 65, 65)
            ro, co = 64 * (i % 2), (i // 2) * 65
            mmKV = R.op("pe", lambda e, lh=lh, rv=rv, ro=ro, co=co, b=bKV: e.matmul(
                out=self.pb[b][ro:ro + 64, co:co + 65], lhsT=lh, rhs=rv,
                start=True, stop=True), inc="pe" if i == 7 else None)

        if self.core_p < 4:
            dS0 = R.op("dve", lambda e, b=bKV: e.tensor_copy(
                out=self.abf(pkt, 260), in_=self.pb[b][:, 0:260]), inc="dve")
            self.use_bank(bKV, dS0)
            return
        for b in bY:
            self.wait_bank("pe", b)
        R.wait("pe", dat)
        R.wait("pe", self.dep_sbf)
        mmY = None
        for i in range(8):
            h = 8 * w + i
            ec, ro = h // 2, 64 * (h % 2)
            b, co = bY[i // 4], (i % 4) * 65
            la = self.abf(at + i * 128, 128)
            rv = self.abf(v_off + nl * 1040 + h * 65, 65)
            R.op("pe", lambda e, la=la, rv=rv, b=b, co=co: e.matmul(
                out=self.pb[b][:, co:co + 65], lhsT=la, rhs=rv, start=True, stop=False))
            lq = self.abf(pq_off + h * 512 + nl * 128, 128)
            rs = self.S_bf[:, h * 65:h * 65 + 65]
            mmY = R.op("pe", lambda e, lq=lq, rs=rs, b=b, co=co: e.matmul(
                out=self.pb[b][:, co:co + 65], lhsT=lq, rhs=rs, start=False, stop=True),
                inc="pe" if i == 7 else None)

        if self.core_p < 5:
            dS0 = R.op("dve", lambda e, b=bKV: e.tensor_copy(
                out=self.abf(pkt, 260), in_=self.pb[b][:, 0:260]), inc="dve")
            self.use_bank(bKV, dS0)
            dY0 = R.op("dve", lambda e, b=bY[0]: e.tensor_copy(
                out=self.abf(yt, 260), in_=self.pb[b][:, 0:260]), inc="dve")
            dY1 = R.op("dve", lambda e, b=bY[1]: e.tensor_copy(
                out=self.abf(yt, 260), in_=self.pb[b][:, 0:260]), inc="dve")
            self.use_bank(bY[0], dY1)
            self.use_bank(bY[1], dY1)
            return
        R.wait("dve", mmKV)
        R.wait("dve", mmY)
        so = 260 * w
        dS = R.op("dve", lambda e, so=so, b=bKV: e.tensor_tensor(
            out=self.S_all[:, so:so + 260], in0=self.S_all[:, so:so + 260],
            in1=self.pb[b][:, 0:260], op=ALU.add), inc="dve")
        self.use_bank(bKV, dS)
        for par in range(2):
            srcv = self.S_all[64 * par:64 * par + 64, so:so + 260].rearrange(
                "p (j c) -> p j c", c=65)
            dstv = self.S_bf[64 * par:64 * par + 64, :].rearrange(
                "p (h c) -> p h c", c=65)[:, 8 * w + par:8 * w + 8:2, :]
            dSb = R.op("dve", lambda e, srcv=srcv, dstv=dstv: e.tensor_copy(
                out=dstv, in_=srcv), inc="dve")
        self.dep_sbf = dSb

        if self.core_p < 6:
            return
        for j in range(2):
            R.op("dve", lambda e, j=j: e.tensor_copy(
                out=self.gsc[:, 4 * j:4 * j + 4],
                in_=self.pb[bY[j]][:, 0:260].rearrange("p (h c) -> p h c", c=65)
                [:, :, 64:65].rearrange("p h c -> p (h c)")), fragile=True)
        R.op("dve", lambda e: e.tensor_scalar(
            out=self.gsc[:, 0:8], in0=self.gsc[:, 0:8], scalar1=EPS_ATT, scalar2=0.0,
            op0=ALU.add, op1=ALU.add), fragile=True)
        R.op("dve", lambda e: e.reciprocal(out=self.gsc[:, 0:8], in_=self.gsc[:, 0:8]),
             fragile=True)
        dyt = None
        for i in range(8):
            b, co = bY[i // 4], (i % 4) * 65
            dyt = R.op("dve", lambda e, b=b, co=co, i=i: e.tensor_scalar(
                out=self.abf(yt + i * 64, 64), in0=self.pb[b][:, co:co + 64],
                scalar1=self.gsc[:, i:i + 1], scalar2=0.0,
                op0=ALU.mult, op1=ALU.add), inc="dve",
                fragile=(i == 0))
        for b in bY:
            self.use_bank(b, dyt)

        if self.core_p < 7:
            return
        tb2 = 6 + (self.tr_turn % 2)
        self.tr_turn += 1
        self.wait_bank("pe", tb2)
        R.wait("pe", dyt)
        dty = None
        for j in range(4):
            dty = R.op("pe", lambda e, j=j, tb2=tb2: e.transpose(
                out=self.bank_bf(tb2)[:, j * 128:j * 128 + 128],
                in_=self.abf(yt + j * 128, 128), identity=self.ident[:]),
                inc="pe" if j == 3 else None)
        R.wait("act", dty)
        dyf = None
        for j in range(4):
            ec = 4 * w + j
            dyf = R.op("act", lambda e, j=j, ec=ec, tb2=tb2: e.activation(
                out=self.abf(yf_off + ec * 512 + nl * 128, 128),
                in_=self.bank_bf(tb2)[:, j * 128:j * 128 + 128], func=AF.Identity),
                inc="act" if j == 3 else None)
        self.use_bank(tb2, dyf)


    def emit_ffn(self, idx):
        R = self.rec
        A = self.arena
        A.push()
        z1_off = A.alloc(2048)
        sq_off = A.alloc(2048)
        jnk_off = A.alloc(2048)
        zb_off = A.alloc(1024)
        xn_off = A.alloc(4096)
        h_off = A.alloc(16384)
        pr_off = A.alloc(16)
        mk_off = A.alloc(16)
        ms_off = A.alloc(16)
        self.zb_rd = None

        for blk in range(NBLK):
            dep_xn = self.ln_block(blk, z1_off, sq_off, jnk_off, zb_off, xn_off,
                                   self.po("gamma"), self.po("beta"),
                                   fold=True, double=True)
            R.wait("pe", self.dep_wg)
            for tl in range(4):
                self.emit_gating(blk, tl, xn_off, pr_off, mk_off, ms_off, dep_xn)
            for ex in range(NEXP):
                self.emit_expert(idx, blk, ex, xn_off, h_off, dep_xn)
            self.barrier()
        A.pop()

    def emit_gating(self, blk, tl, xn_off, pr_off, mk_off, ms_off, dep_xn):
        R = self.rec
        bank = 5
        self.wait_bank("pe", bank)
        R.wait("pe", dep_xn)
        mm = None
        for kc in range(8):
            s = self.abf(xn_off + kc * 512 + tl * 128, 128)
            mm = R.op("pe", lambda e, s=s, kc=kc: e.matmul(
                out=self.pb[bank][:, 0:8], lhsT=s, rhs=self.wg_bf[:, kc * 8:kc * 8 + 8],
                start=(kc == 0), stop=(kc == 7)), inc="pe" if kc == 7 else None)
        st = self.gsc
        c0, c1, c2, c3, c4 = 8, 9, 10, 11, 12
        prob = self.af32(pr_off, 8)
        msk = self.af32(mk_off, 8)
        msd = self.af32(ms_off, 8)
        R.wait("dve", mm)
        d0 = R.op("dve", lambda e: e.reduce_max(
            out=st[:, c0:c0 + 1], in_=self.pb[bank][:, 0:8], axis=AX.X, negate=True),
            inc="dve")
        R.wait("act", d0)
        d1 = R.op("act", lambda e: e.activation(
            out=prob, in_=self.pb[bank][:, 0:8], func=AF.Exp,
            bias=st[:, c0:c0 + 1]), inc="act")
        self.use_bank(bank, d1)
        R.wait("dve", d1)
        R.op("dve", lambda e: e.reduce_sum(out=st[:, c1:c1 + 1], in_=prob, axis=AX.X), fragile=True)
        R.op("dve", lambda e: e.reciprocal(out=st[:, c1:c1 + 1], in_=st[:, c1:c1 + 1]), fragile=True)
        R.op("dve", lambda e: e.tensor_scalar(
            out=prob, in0=prob, scalar1=st[:, c1:c1 + 1], scalar2=0.0,
            op0=ALU.mult, op1=ALU.add), fragile=True)
        R.op("dve", lambda e: e.reduce_max(out=st[:, c2:c2 + 1], in_=prob, axis=AX.X), fragile=True)
        R.op("dve", lambda e: e.tensor_scalar(
            out=msk, in0=prob, scalar1=st[:, c2:c2 + 1], scalar2=0.0,
            op0=ALU.is_equal, op1=ALU.add), fragile=True)
        R.op("dve", lambda e: e.scalar_tensor_tensor(
            out=msd, in0=msk, scalar=-1e30, in1=prob, op0=ALU.mult, op1=ALU.add), fragile=True)
        R.op("dve", lambda e: e.reduce_max(out=st[:, c3:c3 + 1], in_=msd, axis=AX.X), fragile=True)
        R.op("dve", lambda e: e.tensor_scalar(
            out=msd, in0=msd, scalar1=st[:, c3:c3 + 1], scalar2=0.0,
            op0=ALU.is_equal, op1=ALU.add), fragile=True)
        R.op("dve", lambda e: e.tensor_tensor(
            out=st[:, c4:c4 + 1], in0=st[:, c2:c2 + 1], in1=st[:, c3:c3 + 1], op=ALU.add), fragile=True)
        R.op("dve", lambda e: e.tensor_scalar(
            out=st[:, c4:c4 + 1], in0=st[:, c4:c4 + 1], scalar1=1e-9, scalar2=0.0,
            op0=ALU.add, op1=ALU.add), fragile=True)
        R.op("dve", lambda e: e.reciprocal(out=st[:, c4:c4 + 1], in_=st[:, c4:c4 + 1]), fragile=True)
        R.op("dve", lambda e: e.tensor_tensor(
            out=st[:, c2:c2 + 1], in0=st[:, c2:c2 + 1], in1=st[:, c4:c4 + 1], op=ALU.mult), fragile=True)
        R.op("dve", lambda e: e.tensor_tensor(
            out=st[:, c3:c3 + 1], in0=st[:, c3:c3 + 1], in1=st[:, c4:c4 + 1], op=ALU.mult), fragile=True)
        R.op("dve", lambda e: e.tensor_scalar(
            out=msd, in0=msd, scalar1=st[:, c3:c3 + 1], scalar2=0.0,
            op0=ALU.mult, op1=ALU.add), fragile=True)
        go = (blk % 2) * 32 + tl * 8
        dG = R.op("dve", lambda e, go=go: e.scalar_tensor_tensor(
            out=self.G_blk[:, go:go + 8], in0=msk, scalar=st[:, c2:c2 + 1],
            in1=msd, op0=ALU.mult, op1=ALU.add), inc="dve", fragile=True)
        self.dep_G = dG

    def emit_expert(self, idx, blk, ex, xn_off, h_off, dep_xn):
        R = self.rec
        b1c = self.po("b1") + 32 * ex

        def gelu_evac(fc, bank, mm_dep):
            R.wait("act", mm_dep)
            R.wait("act", self.h_rd[fc])
            dd = R.op("act", lambda e, fc=fc, bank=bank: e.activation(
                out=self.abf(h_off + fc * 512, 512), in_=self.pb[bank][:],
                func=AF.Gelu_apprx_tanh,
                bias=self.pfm[:, b1c + fc:b1c + fc + 1]), inc="act")
            self.h_dep[fc] = dd
            return dd
        self.h_dep = getattr(self, "h_dep", [None] * 32)
        self.h_rd = getattr(self, "h_rd", [None] * 32)
        R.wait("pe", dep_xn)
        self.modeA(8, 32, 2,
                   stat_tile=lambda u_off, kc, mc: u_off + (((mc // 2) % 4) * 16 + kc * 2 + mc % 2) * 128,
                   moving=lambda kc: self.abf(xn_off + kc * 512, 512),
                   evac=gelu_evac,
                   ring_units=4,
                   unit_src=lambda u, ex=ex: self.inp["W1_units"][ex][u],
                   tiles_per_unit_grp=4)

        def w2_evac(tl, dh, bank, dep):
            t0 = blk * 4 + tl
            xt = self.xs[:, t0 * 1024 + dh * 512: t0 * 1024 + dh * 512 + 512]
            R.wait("dve", dep)
            R.wait("dve", self.dep_G)
            go = (blk % 2) * 32 + tl * 8 + ex
            dd = R.op("dve", lambda e, xt=xt, bank=bank, go=go: e.scalar_tensor_tensor(
                out=xt, in0=self.pb[bank][:], scalar=self.G_blk[:, go:go + 1],
                in1=xt, op0=ALU.mult, op1=ALU.add), inc="dve")
            self.stream_dep[t0] = dd
            return dd

        def h_stat(kc, tl):
            R.wait("pe", self.h_dep[kc])
            return self.abf(h_off + kc * 512 + tl * 128, 128)
        self.modeB(32, stat=h_stat,
                   moving=lambda u_off, kc, dh: self.ring[:, u_off + (kc % 8) * 1024 + dh * 512:
                                                         u_off + (kc % 8) * 1024 + dh * 512 + 512],
                   t0locs=list(range(4)),
                   bias_cols=(2048 + ex * 1024, 2048 + ex * 1024 + 512),
                   evac=w2_evac, ring_units=4,
                   unit_src=lambda u, ex=ex: self.inp["W2_units"][ex][u], k_per_unit=8)
        for fc in range(32):
            self.h_rd[fc] = R.now("pe")


    def emit_attn_b(self):
        R = self.rec
        A = self.arena
        A.push()
        U = A.alloc(7168)
        z1_off, sq_off, jnk_off = U, U + 2048, U + 4096
        sc = {"uc": U, "sd": U + 2048, "sg": U + 4096, "ar": U + 5120, "vs": U + 6144}
        zb_off = A.alloc(1024)
        zg_off = A.alloc(4096)
        xc_off = A.alloc(8192)
        xg_off = A.alloc(8192)
        xp_off = A.alloc(2 * 1032)
        ca_off = A.alloc(2 * 1024)
        self.zb_rd = None
        self.xc_dep = [None] * 16
        self.xg_dep = [None] * 16
        self.xp_rd = [None, None]
        self.ca_rd = [None, None]
        self.sd_rd = [None, None]
        self.uc_rd = [None, None]
        self.sg_rd = None

        for blk in range(NBLK):
            dep_zg = self.ln_block(blk, z1_off, sq_off, jnk_off, zb_off, zg_off,
                                   self.po("ln_g"), self.po("ln_b"),
                                   fold=True, double=True)
            for ecg in range(8):
                self.emit_winxp_pair(blk, ecg, zg_off, xc_off, xp_off, ca_off, dep_zg)
            for p in range(8):
                self.emit_ssm_pair(blk, p, zg_off, xc_off, xg_off, sc)
            def o_evac(tl, dh, bank, dep):
                t0 = blk * 4 + tl
                xt = self.xs[:, t0 * 1024 + dh * 512: t0 * 1024 + dh * 512 + 512]
                R.wait("dve", dep)
                dd = R.op("dve", lambda e, xt=xt, bank=bank: e.tensor_tensor(
                    out=xt, in0=xt, in1=self.pb[bank][:], op=ALU.add), inc="dve")
                self.stream_dep[t0] = dd
                return dd

            def xg_stat(kc, tl):
                R.wait("pe", self.xg_dep[kc])
                return self.abf(xg_off + kc * 512 + tl * 128, 128)
            self.modeB(16, stat=xg_stat,
                       moving=lambda u_off, kc, dh: self.ring[:, u_off + kc * 1024 + dh * 512:
                                                             u_off + kc * 1024 + dh * 512 + 512],
                       t0locs=list(range(4)), bias_cols=(1024, 1536),
                       evac=o_evac, ring_units=1,
                       unit_src=lambda u: self.inp["WoT_pack"][0], k_per_unit=16)
            self.barrier()
        A.pop()

    def emit_winxp_pair(self, blk, ecg, zg_off, xc_off, xp_off, ca_off, dep_zg):
        R = self.rec
        banks = [0, 1] if ecg % 2 == 0 else [2, 3]
        for b in banks:
            self.wait_bank("pe", b)
        off, udep, slot = self.ring_load(self.inp["WinT_xp"][ecg], 2048)
        R.wait("pe", udep)
        R.wait("pe", dep_zg)
        mm = None
        for kc in range(8):
            for i in range(2):
                t = off + (kc * 2 + i) * 128
                mm = R.op("pe", lambda e, t=t, b=banks[i], kc=kc: e.matmul(
                    out=self.pb[b][:], lhsT=self.ring[:, t:t + 128],
                    rhs=self.abf(zg_off + kc * 512, 512),
                    start=(kc == 0), stop=(kc == 7)),
                    inc="pe" if (kc == 7 and i == 1) else None)
        self.ring_dep[slot] = mm
        cw = self.po("convw")
        for i in range(2):
            ec = 2 * ecg + i
            rot = ec % 2
            xp = self.af32(xp_off + rot * 1032, 516)
            ca = self.af32(ca_off + rot * 1024, 512)
            R.wait("act", mm)
            R.wait("act", self.xp_rd[rot])
            d0 = R.op("act", lambda e, i=i, ec=ec, xp=xp: e.activation(
                out=xp[:, 3:515], in_=self.pb[banks[i]][:], func=AF.Identity,
                bias=self.pfm[:, self.po("b_in") + ec:self.po("b_in") + ec + 1]),
                inc="act")
            self.use_bank(banks[i], d0)
            R.wait("dve", d0)
            R.wait("dve", self.dep_const)
            R.wait("dve", self.ca_rd[rot])
            if blk == 0:
                R.op("dve", lambda e, xp=xp: e.memset(xp[:, 0:3], 0.0))
            else:
                R.op("dve", lambda e, xp=xp, ec=ec: e.tensor_copy(
                    out=xp[:, 0:3], in_=self.xptail[:, ec * 3:ec * 3 + 3]), fragile=True)
            R.op("dve", lambda e, xp=xp, ca=ca, ec=ec, cw=cw: e.tensor_scalar(
                out=ca, in0=xp[:, 0:512], scalar1=self.pfm[:, cw + ec:cw + ec + 1],
                scalar2=0.0, op0=ALU.mult, op1=ALU.add))
            for j in range(1, 4):
                R.op("dve", lambda e, xp=xp, ca=ca, ec=ec, j=j, cw=cw: e.scalar_tensor_tensor(
                    out=ca, in0=xp[:, j:j + 512],
                    scalar=self.pfm[:, cw + 16 * j + ec:cw + 16 * j + ec + 1],
                    in1=ca, op0=ALU.mult, op1=ALU.add))
            dtl = R.op("dve", lambda e, xp=xp, ec=ec: e.tensor_copy(
                out=self.xptail[:, ec * 3:ec * 3 + 3], in_=xp[:, 512:515]), inc="dve")
            self.xp_rd[rot] = dtl
            R.wait("act", dtl)
            dsl = R.op("act", lambda e, ca=ca, ec=ec: e.activation(
                out=self.abf(xc_off + ec * 512, 512), in_=ca, func=AF.Silu,
                bias=self.pfm[:, self.po("conv_b") + ec:self.po("conv_b") + ec + 1]),
                inc="act")
            self.xc_dep[ec] = dsl
            self.ca_rd[rot] = dsl

    def emit_ssm_pair(self, blk, p, zg_off, xc_off, xg_off, sc):
        R = self.rec
        bC = [0, 1] if p % 2 == 0 else [4, 5]
        bD = [2, 3] if p % 2 == 0 else [6, 7]
        for kc in range(16):
            R.wait("pe", self.xc_dep[kc])

        def grp(units, banks, n_k, mov_off):
            for b in banks:
                self.wait_bank("pe", b)
            off, udep, slot = self.ring_load(units, None)
            R.wait("pe", udep)
            mm = None
            for kc in range(n_k):
                for i in range(2):
                    t = off + (kc * 2 + i) * 128
                    mm = R.op("pe", lambda e, t=t, b=banks[i], kc=kc, mo=mov_off, nk=n_k:
                              e.matmul(out=self.pb[b][:], lhsT=self.ring[:, t:t + 128],
                                       rhs=self.abf(mo + kc * 512, 512),
                                       start=(kc == 0), stop=(kc == nk - 1)),
                              inc="pe" if (kc == n_k - 1 and i == 1) else None)
            self.ring_dep[slot] = mm
            return mm

        mmC = grp(self.inp["W_cT_pack"][p], bC, 16, xc_off)
        mmD = grp(self.inp["W_dT_pack"][p], bD, 16, xc_off)
        ducs = []
        for i in range(2):
            fc = 2 * p + i
            uc = self.af32(sc["uc"] + i * 1024, 512)
            R.wait("act", mmC)
            R.wait("act", self.uc_rd[i])
            R.wait("act", self.ln_cast_dep)
            duc = R.op("act", lambda e, uc=uc, fc=fc, i=i: e.activation(
                out=uc, in_=self.pb[bC[i]][:], func=AF.Identity,
                bias=self.pfm[:, self.po("b_c") + fc:self.po("b_c") + fc + 1]),
                inc="act")
            self.use_bank(bC[i], duc)
            ducs.append(duc)
        mmG = grp(self.inp["WinT_gate"][p], bC, 8, zg_off)
        dsds = []
        for i in range(2):
            fc = 2 * p + i
            sd = self.af32(sc["sd"] + i * 1024, 512)
            R.wait("act", mmD)
            R.wait("act", self.sd_rd[i])
            dsd = R.op("act", lambda e, sd=sd, fc=fc, i=i: e.activation(
                out=sd, in_=self.pb[bD[i]][:], func=AF.Sigmoid,
                bias=self.pfm[:, self.po("b_d") + fc:self.po("b_d") + fc + 1]), inc="act")
            self.use_bank(bD[i], dsd)
            dsds.append(dsd)
        for i in range(2):
            fc = 2 * p + i
            uc = self.af32(sc["uc"] + i * 1024, 512)
            sd = self.af32(sc["sd"] + i * 1024, 512)
            sg = self.af32(sc["sg"], 512)
            ar = self.af32(sc["ar"], 512)
            vs = self.af32(sc["vs"], 512)
            R.wait("act", mmG)
            R.wait("act", self.sg_rd)
            dsg = R.op("act", lambda e, sg=sg, fc=fc, i=i: e.activation(
                out=sg, in_=self.pb[bC[i]][:], func=AF.Sigmoid,
                bias=self.pfm[:, self.po("b_in_g") + fc:self.po("b_in_g") + fc + 1]),
                inc="act")
            self.use_bank(bC[i], dsg)
            R.wait("dve", ducs[i])
            R.wait("dve", dsds[i])
            R.wait("dve", dsg)
            R.op("dve", lambda e, uc=uc, sd=sd: e.tensor_tensor(
                out=uc, in0=uc, in1=sd, op=ALU.mult))
            R.op("dve", lambda e, ar=ar: e.memset(ar, 1.0))
            R.op("dve", lambda e, ar=ar, fc=fc: e.tensor_scalar(
                out=ar, in0=ar, scalar1=self.pext[:, fc:fc + 1], scalar2=0.0,
                op0=ALU.mult, op1=ALU.add))
            R.op("dve", lambda e, ar=ar, uc=uc, vs=vs, fc=fc: e.tensor_tensor_scan(
                out=vs, data0=ar, data1=uc, initial=self.vcar[:, fc:fc + 1],
                op0=ALU.mult, op1=ALU.add), fragile=True)
            R.op("dve", lambda e, vs=vs, fc=fc: e.tensor_copy(
                out=self.vcar[:, fc:fc + 1], in_=vs[:, 511:512]), fragile=True)
            dxg = R.op("dve", lambda e, vs=vs, sg=sg, fc=fc: e.scalar_tensor_tensor(
                out=self.abf(xg_off + fc * 512, 512), in0=vs,
                scalar=self.pext[:, 16 + fc:16 + fc + 1], in1=sg,
                op0=ALU.mult, op1=ALU.mult), inc="dve")
            self.xg_dep[fc] = dxg
            self.sg_rd = dxg
            self.sd_rd[i] = dxg
            self.uc_rd[i] = dxg

    def emit_store(self):
        R = self.rec
        for t0 in range(NT):
            R.wait("sp", self.stream_dep[t0])
        outr = self.out_hbm.rearrange("(t p) d -> p t d", p=128)
        xsv = self.xs[:].rearrange("p (t d) -> p t d", t=NT)
        dst = R.op("sp", lambda e: e.dma_start(out=outr, in_=xsv), inc="st", dma=True)
        if self.dbg is not None:
            off, n = self.dbg
            dbg_hbm = self.nc.dram_tensor("dbg", [128, n], BF16,
                                          kind="ExternalOutput").ap()
            for e_ in ("pe", "act", "dve"):
                R.wait("sp", R.now(e_))
            dst = R.op("sp", lambda e: e.dma_start(out=dbg_hbm, in_=self.abf(off, n)),
                       inc="st", dma=True)
        if getattr(self, "dbg_t", None) is not None:
            nm, n = self.dbg_t
            t = getattr(self, nm)
            dbg2_hbm = self.nc.dram_tensor("dbg2", [128, n], F32,
                                           kind="ExternalOutput").ap()
            for e_ in ("pe", "act", "dve"):
                R.wait("sp", R.now(e_))
            dst = R.op("sp", lambda e: e.dma_start(out=dbg2_hbm, in_=t[:, 0:n]),
                       inc="st", dma=True)
        R.wait("sp", dst)



_CACHE = {}


def kernel(**inputs):
    x = np.asarray(inputs["L0_attn_a_x"], np.float32)
    packed = prepare_inputs(inputs)
    key = _UPTO
    if key not in _CACHE:
        _CACHE[key] = Kern(upto=_UPTO).build()
    nc = _CACHE[key]
    common = {k: v for k, v in packed.items() if k != "pfm_layout"}
    in_maps = [dict(common, x=np.ascontiguousarray(x[i])) for i in range(8)]
    res = run_bass_kernel_spmd(nc, in_maps, core_ids=list(range(8)))
    return np.stack([r["out"] for r in res.results]).astype(np.float32)
